# version 13
# baseline (speedup 1.0000x reference)
import math
import numpy as np

import concourse.bass as bass
import concourse.mybir as mybir
from concourse.bass_utils import run_bass_kernel_spmd

F32 = mybir.dt.float32
BF16 = mybir.dt.bfloat16
AF = mybir.ActivationFunctionType
ALU = mybir.AluOpType

B, S, D = 4, 2048, 4096
NQ = 1024
H, DN, DR, DV, KVL, QL = 32, 128, 64, 128, 512, 1536
E, DK, R = 8, 32, 16
SCALE = 1.0 / math.sqrt(float(DN + DR))
SCALE_F = 1.0 / math.sqrt(float(DK))

QSr = [(k % 8) * 128 for k in range(16)]
NWr = [NQ - QSr[k] for k in range(16)]
OFFr = np.concatenate([[0], np.cumsum(NWr)]).astype(int).tolist()
PTOT = OFFr[16]

ENGS = ("tensor", "vector", "scalar", "sync", "gpsimd")


def _round32(x):
    return (x + 31) & ~31


class Arena:
    def __init__(self, cap):
        self.cap = cap
        self.blocks = {}
        self.free = [(0, cap)]
        self.hw = 0

    def alloc(self, name, size, side="left"):
        size = _round32(size)
        idxs = range(len(self.free)) if side == "left" else \
            range(len(self.free) - 1, -1, -1)
        for i in idxs:
            off, fsz = self.free[i]
            if fsz >= size:
                if side == "left":
                    self.blocks[name] = (off, size)
                    if fsz == size:
                        self.free.pop(i)
                    else:
                        self.free[i] = (off + size, fsz - size)
                else:
                    self.blocks[name] = (off + fsz - size, size)
                    if fsz == size:
                        self.free.pop(i)
                    else:
                        self.free[i] = (off, fsz - size)
                self.hw = max(self.hw, self.blocks[name][0] + size)
                return self.blocks[name][0]
        raise RuntimeError(f"SBUF OOM {name} ({size}B) free={self.free}")

    def alloc_at(self, name, off, size):
        size = _round32(size)
        for i, (o, s) in enumerate(self.free):
            if o <= off and off + size <= o + s:
                self.free.pop(i)
                if off > o:
                    self.free.append((o, off - o))
                if o + s > off + size:
                    self.free.append((off + size, o + s - off - size))
                self.free.sort()
                self.blocks[name] = (off, size)
                return off
        raise RuntimeError(f"SBUF alloc_at {name} @{off} unavailable")

    def release(self, name):
        off, size = self.blocks.pop(name)
        self.free.append((off, size))
        self.free.sort()
        merged = []
        for o, s in self.free:
            if merged and merged[-1][0] + merged[-1][1] == o:
                merged[-1][1] += s
            else:
                merged.append([o, s])
        self.free = [tuple(m) for m in merged]


def _nsplits(n0, n1):
    out = []
    a = n0
    while a < n1:
        b2 = min(n1, (a // 512 + 1) * 512)
        out.append((a, b2))
        a = b2
    return out


def build_nc():
    nc = bass.Bass("TRN2", target_bir_lowering=False)

    def din(name, shape, dtype=F32):
        return nc.dram_tensor(name, shape, dtype, kind="ExternalInput").ap()

    xkv = din("xkv", [S, D])
    wqd = din("wqd", [D, QL])
    wqun = din("wqun", [QL, H * DN])
    wqur = din("wqur", [QL, H * DR])
    wkvd = din("wkvd", [D, KVL])
    wkr = din("wkr", [D, DR])
    wuk = din("wuk", [H * DN, KVL])
    wuvT = din("wuvT", [KVL, H * DV])
    wout = din("wout", [H * DV, D])
    WTi = din("WTi", [D, D])
    bvec = din("bvec", [1, D])
    WqT = din("WqT", [D, E * DK])
    WkT = din("WkT", [D, E * DK])
    ATi = din("ATi", [D, E * R])
    Bfl = din("Bfl", [E * R, D])
    ck2 = din("ck2", [128, 16 * 64])
    sk2 = din("sk2", [128, 16 * 64])
    cq2 = din("cq2", [128, NQ])
    sq2 = din("sq2", [128, NQ])
    maski = din("maski", [128, 16 * 128])
    identi = din("identi", [128, 128])
    onesmi = din("onesmi", [128, 128])
    ones1i = din("ones1i", [1, 128])
    segi = din("segi", [E * DK, E])
    expmi = din("expmi", [E, E * R])

    out = nc.dram_tensor("out", [NQ, D], F32, kind="ExternalOutput").ap()

    ql_scr = nc.dram_tensor("ql_scr", [H * (KVL + DR), NQ], BF16).ap()
    wqd_bf = nc.dram_tensor("wqd_bf", [D, QL], BF16).ap()
    g_scr = nc.dram_tensor("g_scr", [NQ, D], F32).ap()

    q = {e: [] for e in ENGS}
    cnt = {}

    with (
        nc.semaphore("sTE") as _sTE,
        nc.semaphore("sVE") as _sVE,
        nc.semaphore("sAC") as _sAC,
        nc.semaphore("sSY") as _sSY,
        nc.semaphore("sSC") as _sSC,
        nc.semaphore("sGL") as _sGL,
        nc.semaphore("sOUT") as _sOUT,
        nc.semaphore("sPO") as _sPO,
        nc.sbuf_tensor("arena", [128, 204 * 1024], mybir.dt.uint8) as arena_t,
        nc.psum_tensor("ps0", [128, 1024], F32) as ps0,
        nc.psum_tensor("ps1", [128, 1024], F32) as ps1,
        nc.psum_tensor("ps2", [128, 1024], F32) as ps2,
        nc.psum_tensor("ps3", [128, 1024], F32) as ps3,
        nc.Block() as block,
    ):
        sems = {"TE": _sTE, "VE": _sVE, "AC": _sAC, "SY": _sSY, "SC": _sSC,
                "GL": _sGL, "OUT": _sOUT, "PO": _sPO}
        PS = [ps0, ps1, ps2, ps3]

        def inc(sem, k=1):
            cnt[sem] = cnt.get(sem, 0) + k
            return cnt[sem]

        def emit(eng, fn, sem=None, k=1):
            if sem is None:
                q[eng].append(fn)
                return None
            q[eng].append(lambda e, f=fn, s=sems[sem], kk=k: f(e).then_inc(s, kk))
            return inc(sem, k)

        def wait(eng, sem, val):
            if val and val > 0:
                q[eng].append(lambda e, s=sems[sem], v=val: e.wait_ge(s, v))

        arena_base = nc.lookup_mloc(arena_t).addr
        ar = Arena(204 * 1024)
        tens = {}

        def carve_at(name, shape, dtype, off):
            per_part = int(np.prod(shape[1:])) * mybir.dt.size(dtype)
            ar.alloc_at(name, off, per_part)
            h = nc.virtual_sbuf_tensor(name, shape, dtype)
            nc._tensor(h.name, h.shape, h.dtype, type="SB")
            mloc = nc.lookup_mloc(h)
            mloc.allocated = True
            mloc.addr = arena_base + off
            tens[name] = h
            return h

        def carve(name, shape, dtype, side="left"):
            per_part = int(np.prod(shape[1:])) * mybir.dt.size(dtype)
            ar.alloc(name, per_part, side=side)
            h = nc.virtual_sbuf_tensor(name, shape, dtype)
            nc._tensor(h.name, h.shape, h.dtype, type="SB")
            mloc = nc.lookup_mloc(h)
            mloc.allocated = True
            mloc.addr = arena_base + ar.blocks[name][0]
            tens[name] = h
            return h

        def free(name):
            ar.release(name)
            del tens[name]

        def gdma(dst, src):
            return emit("gpsimd", lambda e, d=dst, s=src: e.dma_start(d, s),
                        "GL", 16)

        def ydma(dst, src):
            return emit("sync", lambda e, d=dst, s=src: e.dma_start(d, s),
                        "SY", 16)

        def cdma(dst, src):
            return emit("scalar", lambda e, d=dst, s=src: e.dma_start(d, s),
                        "SC", 16)

        wkvd_sb = carve("wkvd_sb", [128, 32 * KVL], BF16)
        wkr_sb = carve("wkr_sb", [128, 32 * DR], BF16)
        ident_sb = carve("ident_sb", [128, 128], BF16, side="right")
        onesm_sb = carve("onesm_sb", [128, 128], BF16, side="right")
        ones1_sb = carve("ones1_sb", [1, 128], BF16, side="right")
        seg_sb = carve("seg_sb", [128, 2 * E], BF16, side="right")
        expm_sb = carve("expm_sb", [E, E * R], BF16, side="right")

        wkvd_ld = gdma(wkvd_sb[:].rearrange("p (a n) -> p a n", a=32),
                       wkvd.rearrange("(a p) n -> p a n", p=128))
        wkr_ld = gdma(wkr_sb[:].rearrange("p (a n) -> p a n", a=32),
                      wkr.rearrange("(a p) n -> p a n", p=128))
        gdma(ident_sb[:], identi[:, :])
        consts_sy0 = None
        gdma(onesm_sb[:], onesmi[:, :])
        gdma(ones1_sb[:], ones1i[:, :])
        gdma(seg_sb[:].rearrange("p (a n) -> p a n", a=2),
             segi.rearrange("(a p) n -> p a n", p=128))
        consts_gl = gdma(expm_sb[:], expmi[:, :])

        xnO = carve("xnO", [128, 32 * NQ], BF16, side="right")
        sibT = [carve(f"sibT{i}", [128, 32 * 128], BF16) for i in range(2)]
        ckv = carve("ckv", [128, 16 * KVL], BF16, side="right")
        kraw = carve("kraw", [128, 16 * DR], F32)
        x_t = [carve(f"x_t{i}", [128, D], F32) for i in range(3)]
        xn_t = [carve(f"xn_t{i}", [128, D], BF16) for i in range(2)]
        stats = carve("stats", [128, 48], F32)
        mvt = carve("mvt", [128, 2], F32)
        nmean = carve("nmean", [128, 1], F32)
        veps = carve("veps", [128, 1], F32)
        sdv = carve("sdv", [128, 1], F32)
        rstd = carve("rstd", [128, 1], F32)

        tr_te = {}
        tr_ac = {}
        ckv_te = {}
        ckv_ac = {}
        gidx = 0
        wait("tensor", "GL", consts_gl)
        ln_rd = []
        for i in range(16):
            bufx = i % 3
            buf = i % 2
            own = i < 8
            if i >= 3:
                wait("sync", "PO", ln_rd[i - 3])
                wait("scalar", "PO", ln_rd[i - 3])
            ld = ydma(x_t[bufx][:, 0:2048],
                      xkv[i * 128:(i + 1) * 128, 0:2048])
            ld2 = cdma(x_t[bufx][:, 2048:4096],
                       xkv[i * 128:(i + 1) * 128, 2048:4096])
            wait("vector", "SY", ld)
            wait("vector", "SC", ld2)
            for c in range(7):
                emit("vector", lambda e, b_=bufx, c_=c: e.bn_stats(
                    stats[:, c_ * 6:(c_ + 1) * 6],
                    x_t[b_][:, c_ * 512:(c_ + 1) * 512]))
            w0 = emit("vector", lambda e, b_=bufx: e.bn_stats(
                stats[:, 42:48], x_t[b_][:, 3584:4096]), "VE")
            wait("vector", "VE", w0)
            w1 = emit("vector", lambda e: e.bn_aggr(mvt[:], stats[:]), "VE")
            wait("vector", "VE", w1)
            emit("vector", lambda e: e.tensor_scalar(
                nmean[:], mvt[:, 0:1], -1.0, None, ALU.mult))
            w2 = emit("vector", lambda e: e.tensor_scalar(
                veps[:], mvt[:, 1:2], 1e-5, None, ALU.add), "VE")
            wait("scalar", "VE", w2)
            w3 = emit("scalar", lambda e: e.activation(sdv[:], veps[:], AF.Sqrt),
                      "AC")
            wait("vector", "AC", w3)
            w4 = emit("vector", lambda e: e.reciprocal(rstd[:], sdv[:]), "VE")
            wait("vector", "VE", w4)
            if i >= 2:
                wait("gpsimd", "TE", tr_te[i - 2])
            wait("gpsimd", "VE", w4)
            w5 = emit("gpsimd", lambda e, b_=buf, bx_=bufx: e.tensor_scalar(
                xn_t[b_][:], x_t[bx_][:], nmean[:], rstd[:], ALU.add,
                ALU.mult), "PO")
            ln_rd.append(w5)
            wait("tensor", "PO", w5)
            if not own and i >= 10:
                wait("scalar", "TE", ckv_te[i - 2])
            for g in range(8):
                slot = gidx % 4
                pt = PS[slot // 2]
                coff = (slot % 2) * 512
                if gidx >= 4:
                    wait("tensor", "AC", tr_ac[gidx - 4])
                for c4 in range(4):
                    ic = g * 4 + c4
                    fn = (lambda e, b_=buf, ic_=ic, c_=c4, p_=pt,
                          o_=coff: e.matmul(
                        p_[:, o_ + c_ * 128:o_ + (c_ + 1) * 128],
                        xn_t[b_][:, ic_ * 128:(ic_ + 1) * 128],
                        ident_sb[:], start=(c_ == 0), stop=(c_ == 3),
                        skip_group_check=True))
                    if c4 == 3:
                        tg = emit("tensor", fn, "TE")
                    else:
                        emit("tensor", fn)
                if g == 7:
                    tr_te[i] = tg
                wait("scalar", "TE", tg)
                if own:
                    fn = (lambda e, g_=g, i_=i, p_=pt, o_=coff: e.copy(
                        xnO[:].rearrange("p (a n) -> p a n", a=32)
                        [:, g_ * 4:(g_ + 1) * 4, i_ * 128:(i_ + 1) * 128],
                        p_[:, o_:o_ + 512]
                        .rearrange("p (a n) -> p a n", a=4)))
                else:
                    fn = (lambda e, g_=g, b_=buf, p_=pt, o_=coff: e.copy(
                        sibT[b_][:, g_ * 512:(g_ + 1) * 512],
                        p_[:, o_:o_ + 512]))
                tr_ac[gidx] = emit("scalar", fn, "AC")
                gidx += 1
            if i == 0:
                wait("tensor", "GL", wkr_ld)
            wait("tensor", "AC", tr_ac[gidx - 1])
            if i >= 2:
                wait("tensor", "AC", ckv_ac[i - 2])
            par2 = i % 2
            for ic in range(32):
                if own:
                    rhs_t = xnO[:, ic * NQ + i * 128:ic * NQ + (i + 1) * 128]
                else:
                    rhs_t = sibT[buf][:, ic * 128:(ic + 1) * 128]
                emit("tensor", lambda e, r_=rhs_t, ic_=ic, p_=par2: e.matmul(
                    PS[2][:, p_ * 512:(p_ + 1) * 512],
                    r_,
                    wkvd_sb[:, ic_ * KVL:(ic_ + 1) * KVL],
                    start=(ic_ == 0), stop=(ic_ == 31)))
                fn = (lambda e, r_=rhs_t, ic_=ic, p_=par2: e.matmul(
                    PS[3][:, p_ * 512:p_ * 512 + 64],
                    r_,
                    wkr_sb[:, ic_ * DR:(ic_ + 1) * DR],
                    start=(ic_ == 0), stop=(ic_ == 31)))
                if ic == 31:
                    ckv_te[i] = emit("tensor", fn, "TE")
                else:
                    emit("tensor", fn)
            wait("scalar", "TE", ckv_te[i])
            emit("scalar", lambda e, i_=i, p_=par2: e.copy(
                ckv[:, i_ * KVL:(i_ + 1) * KVL],
                PS[2][:, p_ * 512:(p_ + 1) * 512]))
            ckv_ac[i] = emit("scalar", lambda e, i_=i, p_=par2: e.copy(
                kraw[:, i_ * DR:(i_ + 1) * DR],
                PS[3][:, p_ * 512:p_ * 512 + 64]), "AC")
        ckv_all_ac = cnt["AC"]

        p1offs = {n: ar.blocks[n][0] for n in ("x_t0", "x_t1", "xn_t0")}
        for n in ("x_t0", "x_t1", "x_t2", "xn_t0", "xn_t1", "stats", "mvt",
                  "nmean", "veps", "sdv", "rstd", "sibT0", "sibT1"):
            free(n)
        wait("gpsimd", "AC", ckv_all_ac)

        ck2_sb = carve("ck2_sb", [128, 16 * 64], F32)
        sk2_sb = carve("sk2_sb", [128, 16 * 64], F32)
        wait("sync", "AC", ckv_all_ac)
        ydma(ck2_sb[:], ck2[:, :])
        consts_sy = ydma(sk2_sb[:], sk2[:, :])
        krN = carve("krN", [128, 16 * DR], BF16)
        krope = carve("krope", [64, S], BF16, side="right")
        ckvT = carve("ckvT", [128, 4 * S], BF16, side="right")
        vbuf = carve("vbuf", [128, 16 * DR], F32)

        kraw3 = kraw[:].rearrange("p (a d) -> p a d", a=16)
        vbuf3 = vbuf[:].rearrange("p (a d) -> p a d", a=16)
        krN3 = krN[:].rearrange("p (a d) -> p a d", a=16)
        wait("vector", "AC", ckv_all_ac)
        wait("vector", "SY", consts_sy)
        emit("vector", lambda e: e.tensor_tensor(vbuf[:], kraw[:], sk2_sb[:],
                                                 ALU.mult))
        w6 = emit("vector", lambda e: e.tensor_tensor(
            kraw[:], kraw[:], ck2_sb[:], ALU.mult), "VE")
        wait("vector", "VE", w6)
        emit("vector", lambda e: e.tensor_tensor(
            krN3[:, :, 0:32], kraw3[:, :, 0:32], vbuf3[:, :, 32:64],
            ALU.subtract))
        krN_ve = emit("vector", lambda e: e.tensor_tensor(
            krN3[:, :, 32:64], vbuf3[:, :, 0:32], kraw3[:, :, 32:64],
            ALU.add), "VE")

        wait("tensor", "VE", krN_ve)
        kt_ac = {}
        for i in range(16):
            pt2 = PS[i % 2]
            if i >= 2:
                wait("tensor", "AC", kt_ac[i - 2])
            fn = (lambda e, i_=i, p_=pt2: e.matmul(
                p_[0:64, 0:128],
                krN[:, i_ * DR:(i_ + 1) * DR],
                ident_sb[:], start=True, stop=True))
            ktd = emit("tensor", fn, "TE")
            wait("scalar", "TE", ktd)
            kt_ac[i] = emit("scalar", lambda e, i_=i, p_=pt2: e.copy(
                krope[0:64, i_ * 128:(i_ + 1) * 128], p_[0:64, 0:128]), "AC")

        krT_te = cnt["TE"]
        ck_ac = {}
        for tc in range(16):
            for rm in range(4):
                gi2 = tc * 4 + rm
                slot = gi2 % 4
                pt2 = PS[2 + slot // 2]
                coff = (slot % 2) * 512
                if gi2 >= 4:
                    wait("tensor", "AC", ck_ac[gi2 - 4])
                fn = (lambda e, t_=tc, r_=rm, p_=pt2, o_=coff: e.matmul(
                    p_[:, o_:o_ + 128],
                    ckv[:, t_ * KVL + r_ * 128:t_ * KVL + (r_ + 1) * 128],
                    ident_sb[:], start=True, stop=True))
                ckd = emit("tensor", fn, "TE")
                wait("scalar", "TE", ckd)
                ck_ac[gi2] = emit("scalar", lambda e, t_=tc, r_=rm, p_=pt2,
                                  o_=coff: e.copy(
                    ckvT[:, r_ * S + t_ * 128:r_ * S + (t_ + 1) * 128],
                    p_[:, o_:o_ + 128]), "AC")
        p1_ac = cnt["AC"]

        free("krN")
        free("kraw")
        free("vbuf")
        free("wkvd_sb")
        free("wkr_sb")
        free("ck2_sb")
        free("sk2_sb")
        xc_ve = None

        wqd_sb = [carve_at(f"wqd_sb{i}", [128, 4 * QL], BF16, p1offs[n])
                  for i, n in ((0, "x_t0"), (1, "x_t1"))]
        wait("gpsimd", "PO", ln_rd[15])
        wait("gpsimd", "VE", cnt["VE"])
        wait("gpsimd", "TE", krT_te)
        cqT_sb = carve("cqT_sb", [128, 12 * NQ], BF16)
        wqT_sb = carve("wqT_sb", [128, 32 * 256], BF16)
        wkT_sb = carve("wkT_sb", [128, 32 * 256], BF16)
        at_sb = carve("at_sb", [128, 32 * 128], BF16)

        wait("tensor", "AC", p1_ac)
        te_e8 = {}
        cq_ev = {}
        wb_sy = {}
        for grp in range(3):
            for e8 in range(8):
                bufq = e8 % 2
                prev = (grp, e8 - 2) if e8 >= 2 else \
                       ((grp - 1, 6 + e8) if grp >= 1 else None)
                if grp == 0:
                    if prev is not None:
                        wait("gpsimd", "TE", te_e8[prev])
                    if e8 >= 2:
                        wait("gpsimd", "SY", wb_sy[e8 - 2])
                    ld = gdma(wqd_sb[bufq][:]
                              .rearrange("p (a n) -> p a n", a=4),
                              wqd[(e8) * 512:(e8 + 1) * 512, :]
                              .rearrange("(a p) n -> p a n", p=128))
                    wait("tensor", "GL", ld)
                    wait("sync", "GL", ld)
                    wb_sy[e8] = ydma(
                        wqd_bf[e8 * 512:(e8 + 1) * 512, :]
                        .rearrange("(a p) n -> p a n", p=128),
                        wqd_sb[bufq][:].rearrange("p (a n) -> p a n", a=4))
                else:
                    if prev is not None:
                        wait("sync", "TE", te_e8[prev])
                    if grp == 1 and e8 < 2:
                        wait("sync", "SY", wb_sy[e8])
                    ld = ydma(wqd_sb[bufq][:]
                              .rearrange("p (a n) -> p a n", a=4),
                              wqd_bf[e8 * 512:(e8 + 1) * 512, :]
                              .rearrange("(a p) n -> p a n", p=128))
                    wait("tensor", "SY", ld)
                if grp >= 1 and e8 == 0:
                    wait("tensor", "VE", cq_ev[grp - 1])
                for i4 in range(4):
                    i = e8 * 4 + i4
                    for lg in range(4):
                        lm = grp * 4 + lg
                        for nn2 in range(2):
                            fn = (lambda e, b_=bufq, i4_=i4, i_=i, l_=lm,
                                  g_=lg, n_=nn2: e.matmul(
                                PS[g_][:, n_ * 512:(n_ + 1) * 512],
                                wqd_sb[b_][:, i4_ * QL + l_ * 128:
                                           i4_ * QL + l_ * 128 + 128],
                                xnO[:, i_ * NQ + n_ * 512:
                                    i_ * NQ + n_ * 512 + 512],
                                start=(i_ == 0), stop=(i_ == 31)))
                            if i4 == 3 and lg == 3 and nn2 == 1:
                                te_e8[(grp, e8)] = emit("tensor", fn, "TE")
                            else:
                                emit("tensor", fn)
                if grp == 0 and e8 >= 2:
                    pass
            wait("vector", "TE", te_e8[(grp, 7)])
            for lg in range(4):
                ev = emit("vector", lambda e, g_=lg, gr_=grp: e.tensor_copy(
                    cqT_sb[:, (gr_ * 4 + g_) * NQ:(gr_ * 4 + g_ + 1) * NQ],
                    PS[g_][:]),
                    "VE" if lg == 3 else None)
            cq_ev[grp] = ev
        cq_last_ve = cnt["VE"]

        free("wqd_sb0")
        free("wqd_sb1")

        qf_sb = carve("qf_sb", [128, NQ], F32)
        p_sb = carve("p_sb", [128, 2 * NQ], BF16)
        wf_sb = carve("wf_sb", [128, 64], BF16)
        wft_sb = carve("wft_sb", [8, NQ], BF16)
        hT_sb = carve("hT_sb", [128, NQ], F32)
        hwT_sb = carve("hwT_sb", [128, NQ], BF16, side="right")
        negmax = carve("negmax", [128, 1], F32)
        biasF = carve("biasF", [128, 1], F32)
        wfu_sb = carve("wfu_sb", [128, 8], F32)
        sumF = carve("sumF", [128, 1], F32)
        recF = carve("recF", [128, 1], F32)

        ldq = gdma(wqT_sb[:].rearrange("p (a n) -> p a n", a=32),
                   WqT.rearrange("(a p) n -> p a n", p=128))
        ldk = gdma(wkT_sb[:].rearrange("p (a n) -> p a n", a=32),
                   WkT.rearrange("(a p) n -> p a n", p=128))
        lda = gdma(at_sb[:].rearrange("p (a n) -> p a n", a=32),
                   ATi.rearrange("(a p) n -> p a n", p=128))

        wait("tensor", "GL", ldq)
        wait("tensor", "VE", cq_last_ve)
        pq_te = {}
        for nm in range(2):
            if nm == 1:
                wait("tensor", "VE", pq_te[0])
            for i in range(32):
                for nn2 in range(2):
                    fn = (lambda e, i_=i, m_=nm, n_=nn2: e.matmul(
                        PS[0][:, n_ * 512:(n_ + 1) * 512],
                        wqT_sb[:, i_ * 256 + m_ * 128:i_ * 256 + m_ * 128 + 128],
                        xnO[:, i_ * NQ + n_ * 512:i_ * NQ + n_ * 512 + 512],
                        start=(i_ == 0), stop=(i_ == 31)))
                    if i == 31 and nn2 == 1:
                        tq_ = emit("tensor", fn, "TE")
                    else:
                        emit("tensor", fn)
            wait("vector", "TE", tq_)
            evq = emit("vector", lambda e: e.tensor_copy(qf_sb[:], PS[0][:]),
                       "VE")
            if nm == 0:
                wait("tensor", "GL", ldk)
            wait("tensor", "VE", evq)
            for i in range(32):
                for nn2 in range(2):
                    fn = (lambda e, i_=i, m_=nm, n_=nn2: e.matmul(
                        PS[0][:, n_ * 512:(n_ + 1) * 512],
                        wkT_sb[:, i_ * 256 + m_ * 128:i_ * 256 + m_ * 128 + 128],
                        xnO[:, i_ * NQ + n_ * 512:i_ * NQ + n_ * 512 + 512],
                        start=(i_ == 0), stop=(i_ == 31)))
                    if i == 31 and nn2 == 1:
                        tk_ = emit("tensor", fn, "TE")
                    else:
                        emit("tensor", fn)
            wait("vector", "TE", tk_)
            pq_te[nm] = emit("vector", lambda e, m_=nm: e.tensor_tensor(
                p_sb[:, m_ * NQ:(m_ + 1) * NQ], PS[0][:], qf_sb[:], ALU.mult),
                "VE")

        wait("tensor", "VE", pq_te[1])
        exp_done_f = {}
        for qc in range(8):
            pti = 1 + (qc % 2)
            if qc >= 2:
                wait("tensor", "AC", exp_done_f[qc - 2])
            for nm in range(2):
                fn = (lambda e, m_=nm, q_=qc, p_=pti: e.matmul(
                    PS[p_][:, 0:8],
                    p_sb[:, m_ * NQ + q_ * 128:m_ * NQ + (q_ + 1) * 128],
                    seg_sb[:, m_ * 8:(m_ + 1) * 8],
                    start=(m_ == 0), stop=(m_ == 1)))
                if nm == 1:
                    tsf = emit("tensor", fn, "TE")
                else:
                    emit("tensor", fn)
            wait("vector", "TE", tsf)
            w7 = emit("vector", lambda e, p_=pti: e.tensor_reduce(
                negmax[:], PS[p_][:, 0:8], mybir.AxisListType.X, ALU.max,
                negate=True), "VE")
            wait("vector", "VE", w7)
            w8 = emit("vector", lambda e: e.tensor_scalar(
                biasF[:], negmax[:], SCALE_F, None, ALU.mult), "VE")
            wait("scalar", "VE", w8)
            exp_done_f[qc] = emit("scalar", lambda e, p_=pti: e.activation(
                wfu_sb[:], PS[p_][:, 0:8], AF.Exp, bias=biasF[:],
                scale=SCALE_F, accum_out=sumF[:]), "AC")
            wait("vector", "AC", exp_done_f[qc])
            w9 = emit("vector", lambda e: e.reciprocal(recF[:], sumF[:]), "VE")
            wait("vector", "VE", w9)
            emit("vector", lambda e, q_=qc: e.tensor_scalar(
                wf_sb[:, q_ * 8:(q_ + 1) * 8], wfu_sb[:], recF[:], None,
                ALU.mult), "VE")
        wf_all_ve = cnt["VE"]

        wait("tensor", "VE", wf_all_ve)
        for qc in range(8):
            fn = (lambda e, q_=qc: e.matmul(
                PS[3][0:8, q_ * 128:(q_ + 1) * 128],
                wf_sb[:, q_ * 8:(q_ + 1) * 8],
                ident_sb[:], start=(q_ % 4 == 0), stop=(q_ % 4 == 3),
                skip_group_check=True))
            if qc == 7:
                twf = emit("tensor", fn, "TE")
            else:
                emit("tensor", fn)
        wait("vector", "TE", twf)
        wft_ve = emit("vector", lambda e: e.tensor_copy(wft_sb[:], PS[3][0:8, :]),
                      "VE")

        wait("tensor", "VE", wft_ve)
        for nn2 in range(2):
            fn = (lambda e, n_=nn2: e.matmul(
                PS[3][:, n_ * 512:(n_ + 1) * 512],
                expm_sb[:],
                wft_sb[0:8, n_ * 512:(n_ + 1) * 512],
                start=True, stop=True))
            if nn2 == 1:
                twb = emit("tensor", fn, "TE")
            else:
                emit("tensor", fn)
        wait("tensor", "GL", lda)
        for i in range(32):
            for nn2 in range(2):
                fn = (lambda e, i_=i, n_=nn2: e.matmul(
                    PS[0][:, n_ * 512:(n_ + 1) * 512],
                    at_sb[:, i_ * 128:(i_ + 1) * 128],
                    xnO[:, i_ * NQ + n_ * 512:i_ * NQ + n_ * 512 + 512],
                    start=(i_ == 0), stop=(i_ == 31)))
                if i == 31 and nn2 == 1:
                    tht = emit("tensor", fn, "TE")
                else:
                    emit("tensor", fn)
        wait("vector", "TE", tht)
        hev = emit("vector", lambda e: e.tensor_copy(hT_sb[:], PS[0][:]), "VE")
        wait("vector", "TE", twb)
        wait("vector", "VE", hev)
        hw_ve = emit("vector", lambda e: e.tensor_tensor(
            hwT_sb[:], hT_sb[:], PS[3][:], ALU.mult), "VE")

        for n in ("wqT_sb", "wkT_sb", "at_sb", "qf_sb", "p_sb", "wf_sb",
                  "wft_sb", "hT_sb", "negmax", "biasF", "wfu_sb", "sumF",
                  "recF"):
            free(n)

        qla_sb = [carve(f"qla_sb{i}", [128, 4 * NQ], BF16) for i in range(2)]
        vtmp = carve("vtmp", [128, NQ], F32)
        cq2_sb = carve("cq2_sb", [128, NQ], F32)
        sq2_sb = carve("sq2_sb", [128, NQ], F32)
        wqun_sb = [carve(f"wqun_sb{i}", [128, 12 * 256], BF16) for i in range(4)]
        wqur_sb = [carve(f"wqur_sb{i}", [128, 12 * 256], BF16) for i in range(2)]
        wuk_sb = [carve(f"wuk_sb{i}", [128, 2 * 512], BF16) for i in range(2)]
        qno_sb = [carve(f"qno_sb{i}", [128, NQ], BF16) for i in range(2)]
        qro_sb = carve("qro_sb", [128, NQ], BF16)

        wait("sync", "VE", hw_ve)
        wait("gpsimd", "VE", hw_ve)
        ydma(cq2_sb[:], cq2[:, :])
        ld_tab = ydma(sq2_sb[:], sq2[:, :])

        wait("tensor", "VE", hw_ve)

        nld = {}
        rld = {}
        kld = {}
        qn_te = {}
        qr_te = {}
        uk_te = {}
        scr_sy = {}
        rope_ve = {}
        la_ve = {}
        la_prev = {}
        la_cur = {}
        evn_b = {}
        qno_last = {}
        qla_sy = {}
        for hp in range(16):
            h0, h1 = 2 * hp, 2 * hp + 1
            bufp = hp % 4
            if hp >= 4:
                wait("gpsimd", "TE", qn_te[hp - 4])
            nld[hp] = gdma(wqun_sb[bufp][:].rearrange("p (a n) -> p a n", a=12),
                           wqun[0:QL, hp * 256:(hp + 1) * 256]
                           .rearrange("(a p) n -> p a n", p=128))
            if hp >= 2:
                wait("gpsimd", "TE", uk_te[hp - 2])
            kld[hp] = gdma(wuk_sb[hp % 2][:]
                           .rearrange("p (a n) -> p a n", a=2),
                           wuk[hp * 256:(hp + 1) * 256, :]
                           .rearrange("(a p) n -> p a n", p=128))
            if hp % 2 == 0:
                bufr = (hp // 2) % 2
                if hp >= 4:
                    wait("gpsimd", "TE", qr_te[hp - 3])
                rld[hp] = gdma(wqur_sb[bufr][:]
                               .rearrange("p (a n) -> p a n", a=12),
                               wqur[0:QL, hp * 128:(hp + 2) * 128]
                               .rearrange("(a p) n -> p a n", p=128))


            rbase = (hp // 2) * 2
            bufr = (hp // 2) % 2
            rp = hp % 2
            wait("tensor", "GL", rld[rbase])
            if hp >= 2:
                wait("tensor", "VE", rope_ve[hp - 2])
            for li in range(12):
                for nn2 in range(2):
                    fn = (lambda e, l_=li, n_=nn2, p_=hp, b_=bufr, r_=rp:
                          e.matmul(
                        PS[r_][:, n_ * 512:(n_ + 1) * 512],
                        wqur_sb[b_][:, l_ * 256 + (p_ % 2) * 128:
                                    l_ * 256 + (p_ % 2) * 128 + 128],
                        cqT_sb[:, l_ * NQ + n_ * 512:l_ * NQ + n_ * 512 + 512],
                        start=(l_ == 0), stop=(l_ == 11)))
                    if li == 11 and nn2 == 1:
                        qr_te[hp] = emit("tensor", fn, "TE")
                    else:
                        emit("tensor", fn)
            if hp == 0:
                wait("vector", "SY", ld_tab)
            wait("vector", "TE", qr_te[hp])
            if hp >= 1:
                wait("vector", "SY", scr_sy[hp - 1])
            emit("vector", lambda e, r_=rp: e.tensor_tensor(
                vtmp[:], PS[r_][:], sq2_sb[:], ALU.mult))
            w10 = emit("vector", lambda e, r_=rp: e.tensor_tensor(
                PS[r_][:], PS[r_][:], cq2_sb[:], ALU.mult), "VE")
            wait("vector", "VE", w10)
            for bb in range(2):
                o = bb * 64
                emit("vector", lambda e, o_=o, r_=rp: e.tensor_tensor(
                    qro_sb[o_:o_ + 32, :], PS[r_][o_:o_ + 32, :],
                    vtmp[o_ + 32:o_ + 64, :], ALU.subtract))
                if bb == 1:
                    rope_ve[hp] = emit("vector", lambda e, o_=o, r_=rp:
                                       e.tensor_tensor(
                        qro_sb[o_ + 32:o_ + 64, :], vtmp[o_:o_ + 32, :],
                        PS[r_][o_ + 32:o_ + 64, :], ALU.add), "VE")
                else:
                    emit("vector", lambda e, o_=o, r_=rp: e.tensor_tensor(
                        qro_sb[o_ + 32:o_ + 64, :], vtmp[o_:o_ + 32, :],
                        PS[r_][o_ + 32:o_ + 64, :], ALU.add))

            for bb, hh2 in ((0, h0), (1, h1)):
                pq = 2 + bb
                wait("tensor", "GL", nld[hp])
                if la_prev.get(pq):
                    wait("tensor", "AC", la_prev[pq])
                for li in range(12):
                    for nn2 in range(2):
                        fn = (lambda e, l_=li, n_=nn2, b_=bufp, c_=bb,
                              p_=pq: e.matmul(
                            PS[p_][:, n_ * 512:(n_ + 1) * 512],
                            wqun_sb[b_][:, l_ * 256 + c_ * 128:
                                        l_ * 256 + c_ * 128 + 128],
                            cqT_sb[:, l_ * NQ + n_ * 512:
                                   l_ * NQ + n_ * 512 + 512],
                            start=(l_ == 0), stop=(l_ == 11)))
                        if li == 11 and nn2 == 1:
                            tqn = emit("tensor", fn, "TE")
                        else:
                            emit("tensor", fn)
                if bb == 1:
                    qn_te[hp] = tqn
                wait("scalar", "TE", tqn)
                if qno_last.get(bb):
                    wait("scalar", "TE", qno_last[bb])
                evn_b[bb] = emit("scalar", lambda e, c_=bb, p_=pq: e.copy(
                    qno_sb[c_][:], PS[p_][:]), "AC")
            bufk = hp % 2
            wait("tensor", "GL", kld[hp])
            for bb, hh2 in ((0, h0), (1, h1)):
                wait("tensor", "AC", evn_b[bb])
                for rm in range(4):
                    pr = 2 + ((rm + bb) % 2)
                    if rm >= 2:
                        wait("tensor", "AC", la_cur[(bb, rm - 2)])
                    elif bb == 0 and rm == 1:
                        wait("tensor", "AC", evn_b[1])
                    elif bb == 1 and rm == 0:
                        wait("tensor", "AC", la_cur[(0, 3)])
                    elif bb == 1 and rm == 1:
                        wait("tensor", "AC", la_cur[(0, 2)])
                    for nn2 in range(2):
                        fn = (lambda e, r_=rm, n_=nn2, b_=bufk, h_=hh2,
                              p_=pr, c_=bb: e.matmul(
                            PS[p_][:, n_ * 512:(n_ + 1) * 512],
                            wuk_sb[b_][:, (h_ % 2) * 512 + r_ * 128:
                                       (h_ % 2) * 512 + r_ * 128 + 128],
                            qno_sb[c_][:, n_ * 512:n_ * 512 + 512],
                            start=True, stop=True))
                        if nn2 == 1:
                            tla = emit("tensor", fn, "TE")
                        else:
                            emit("tensor", fn)
                    if rm == 3:
                        qno_last[bb] = tla
                    if bb == 1 and rm == 3:
                        uk_te[hp] = tla
                    wait("scalar", "TE", tla)
                    if rm == 0 and qla_sy.get(bb):
                        wait("scalar", "SY", qla_sy[bb])
                    la_cur[(bb, rm)] = emit(
                        "scalar", lambda e, r_=rm, p_=pr, c_=bb: e.copy(
                            qla_sb[c_][:, r_ * NQ:(r_ + 1) * NQ],
                            PS[p_][:]), "AC")
                if bb == 1:
                    la_prev[2] = la_cur[(1, 3)]
                    la_prev[3] = la_cur[(1, 2)]
                wait("sync", "AC", la_cur[(bb, 3)])
                wait("sync", "AC", la_cur[(bb, 2)])
                qla_sy[bb] = ydma(
                    ql_scr[hh2 * 576:hh2 * 576 + 512, :]
                    .rearrange("(a p) n -> p a n", p=128),
                    qla_sb[bb][:].rearrange("p (a n) -> p a n", a=4))
                wait("sync", "VE", rope_ve[hp])
                scr_sy[hp] = ydma(ql_scr[hh2 * 576 + 512:hh2 * 576 + 576, :],
                                  qro_sb[bb * 64:(bb + 1) * 64, :])
        p2b_sy = cnt["SY"]

        p2b_ac = cnt["AC"]
        free("cqT_sb")
        for n in ("cq2_sb", "sq2_sb", "wqun_sb0", "wqun_sb1",
                  "wqun_sb2", "wqun_sb3",
                  "wqur_sb0", "wqur_sb1", "wuk_sb0", "wuk_sb1", "qno_sb0",
                  "qno_sb1", "qla_sb0", "qla_sb1", "qro_sb", "vtmp"):
            free(n)

        wait("gpsimd", "SY", p2b_sy)
        wait("scalar", "SY", p2b_sy)
        bfl_sb = carve("bfl_sb", [128, D], BF16)
        b_sb = carve("b_sb", [1, D], BF16)
        wt_sb = [carve(f"wt_sb{i}", [128, 32 * 512], BF16) for i in range(2)]
        g_t = [carve(f"g_t{i}", [128, 512], F32) for i in range(2)]

        ldb1 = gdma(bfl_sb[:], Bfl[:, :])
        ldb2 = gdma(b_sb[:], bvec[:, :])
        wait("tensor", "GL", ldb2)
        wait("tensor", "VE", hw_ve)
        wait("tensor", "VE", rope_ve[15])
        wait("tensor", "AC", p2b_ac)

        wt_te = {}
        sil_ac = {}
        gdma_sc = {}
        for og in range(8):
            bufw = og % 2
            if og >= 2:
                wait("gpsimd", "TE", wt_te[og - 2])
            ldw = gdma(wt_sb[bufw][:].rearrange("p (a n) -> p a n", a=32),
                       WTi[0:D, og * 512:(og + 1) * 512]
                       .rearrange("(a p) n -> p a n", p=128))
            wait("tensor", "GL", ldw)
            for j in range(8):
                pj = j % 4
                if og * 8 + j >= 4:
                    prev_id = og * 8 + j - 4
                    wait("tensor", "AC", sil_ac[prev_id])
                for i in range(32):
                    emit("tensor", lambda e, b_=bufw, i_=i, j_=j, p_=pj:
                         e.matmul(
                        PS[p_][:, 0:512],
                        xnO[:, i_ * NQ + j_ * 128:i_ * NQ + (j_ + 1) * 128],
                        wt_sb[b_][:, i_ * 512:(i_ + 1) * 512],
                        start=(i_ == 0), stop=False))
                emit("tensor", lambda e, o_=og, p_=pj: e.matmul(
                    PS[p_][:, 0:512],
                    ones1_sb[:],
                    b_sb[0:1, o_ * 512:(o_ + 1) * 512],
                    start=False, stop=False))
                fn = (lambda e, o_=og, j_=j, p_=pj: e.matmul(
                    PS[p_][:, 0:512],
                    hwT_sb[:, j_ * 128:(j_ + 1) * 128],
                    bfl_sb[:, o_ * 512:(o_ + 1) * 512],
                    start=False, stop=True))
                tid = emit("tensor", fn, "TE")
                if j == 7:
                    wt_te[og] = tid
                wait("scalar", "TE", tid)
                gid = og * 8 + j
                if gid >= 2:
                    wait("scalar", "SC", gdma_sc[gid - 2])
                sil_ac[gid] = emit("scalar", lambda e, b_=gid % 2, p_=pj:
                                   e.activation(g_t[b_][:], PS[p_][:, 0:512],
                                                AF.Silu), "AC")
                wait("scalar", "AC", sil_ac[gid])
                gdma_sc[gid] = cdma(g_scr[j * 128:(j + 1) * 128,
                                          og * 512:(og + 1) * 512],
                                    g_t[gid % 2][:])
        g_all_sc = cnt["SC"]

        free("wt_sb0")
        free("wt_sb1")
        free("g_t0")
        free("g_t1")
        free("bfl_sb")
        free("b_sb")
        free("hwT_sb")
        free("xnO")

        wait("gpsimd", "SC", g_all_sc)
        attnT = carve("attnT", [128, H * NQ], BF16, side="right")
        probsT2 = [carve(f"probsT{i}", [128, PTOT], BF16, side="right")
                   for i in range(2)]
        mask_sb = carve("mask_sb", [128, 16 * 128], BF16, side="right")
        olat_sb = carve("olat_sb", [128, 4 * NQ], BF16)
        recip_sb = carve("recip_sb", [128, NQ], F32)
        lnden = carve("lnden", [128, NQ], F32)
        qla3 = [carve(f"qla3_{i}", [128, 4 * NQ], BF16) for i in range(2)]
        qro3 = [carve(f"qro3_{i}", [64, NQ], BF16) for i in range(2)]
        wuv_sb = [carve(f"wuv_sb{i}", [128, 4 * 512], BF16) for i in range(2)]

        ld_mask = gdma(mask_sb[:], maski[:, :])
        wait("sync", "SC", g_all_sc)

        KORD = [0, 8, 1, 9, 2, 10, 3, 11, 4, 12, 5, 13, 6, 14, 7, 15]
        nmm = []
        for k in range(16):
            for (n0, n1) in _nsplits(QSr[k], NQ):
                nmm.append((k, n0, n1))

        ql_sy = {}
        uv_gl = {}
        sc_te = {}
        exp_ac = {}
        msk_ve = {}
        dn_te = {}
        rec_ve = {}
        oh_te = {}
        at_ve = {}

        def emit_sc(h):
            bufh = h % 2
            SPb = 2 * (h % 2)
            pb = h % 2
            if h >= 2:
                wait("sync", "TE", sc_te[(h - 2, KORD[15])])
            ydma(qla3[bufh][:].rearrange("p (a n) -> p a n", a=4),
                 ql_scr[h * 576:h * 576 + 512, :]
                 .rearrange("(a p) n -> p a n", p=128))
            ql_sy[h] = ydma(qro3[bufh][:],
                            ql_scr[h * 576 + 512:h * 576 + 576, :])
            if h % 4 == 0:
                bufu = (h // 4) % 2
                if h >= 8:
                    wait("gpsimd", "TE", oh_te[h - 5])
                uv_gl[h] = gdma(wuv_sb[bufu][:]
                                .rearrange("p (a n) -> p a n", a=4),
                                wuvT[0:KVL, h * 128:(h + 4) * 128]
                                .rearrange("(a p) n -> p a n", p=128))
            wait("tensor", "SY", ql_sy[h])
            if h == 0:
                wait("tensor", "GL", ld_mask)
                wait("tensor", "AC", sil_ac[63])
            elif h == 1:
                pass
            else:
                pass
            for j, k in enumerate(KORD):
                pk = SPb + (j % 2)
                if h >= 2 and j == 0:
                    wait("tensor", "AC", rec_ve[h - 2])
                if h >= 2 and j == 1:
                    wait("tensor", "VE", at_ve[h - 2])
                if j >= 2:
                    wait("tensor", "AC", exp_ac[(h, KORD[j - 2])])
                mmlist = []
                for rm in range(4):
                    for (n0, n1) in _nsplits(QSr[k], NQ):
                        mmlist.append((rm, n0, n1, False))
                for (n0, n1) in _nsplits(QSr[k], NQ):
                    mmlist.append((4, n0, n1, True))
                for idx, (rm, n0, n1, isrope) in enumerate(mmlist):
                    last = idx == len(mmlist) - 1
                    if isrope:
                        fn = (lambda e, k_=k, n0_=n0, n1_=n1, p_=pk, b_=bufh:
                              e.matmul(
                            PS[p_][0:128, n0_:n1_],
                            krope[:, k_ * 128:(k_ + 1) * 128],
                            qro3[b_][0:64, n0_:n1_],
                            start=False, stop=True))
                    else:
                        fn = (lambda e, k_=k, r_=rm, n0_=n0, n1_=n1, p_=pk,
                              b_=bufh: e.matmul(
                            PS[p_][:, n0_:n1_],
                            ckvT[:, r_ * S + k_ * 128:r_ * S + (k_ + 1) * 128],
                            qla3[b_][:, r_ * NQ + n0_:r_ * NQ + n1_],
                            start=(r_ == 0), stop=False))
                    if last:
                        sc_te[(h, k)] = emit("tensor", fn, "TE")
                    else:
                        emit("tensor", fn)
                wait("scalar", "TE", sc_te[(h, k)])
                if j == 0 and h >= 2:
                    wait("scalar", "TE", dn_te[h - 2])
                exp_ac[(h, k)] = emit("scalar", lambda e, k_=k, p_=pk, b_=pb:
                                      e.activation(
                    probsT2[b_][:, OFFr[k_]:OFFr[k_] + NWr[k_]],
                    PS[p_][:, QSr[k_]:NQ], AF.Exp, scale=SCALE), "AC")
                wait("vector", "AC", exp_ac[(h, k)])
                msk_ve[(h, k)] = emit("vector", lambda e, k_=k, b_=pb:
                                      e.tensor_tensor(
                    probsT2[b_][:, OFFr[k_]:OFFr[k_] + 128],
                    probsT2[b_][:, OFFr[k_]:OFFr[k_] + 128],
                    mask_sb[:, k_ * 128:(k_ + 1) * 128], ALU.mult), "VE")

        def emit_ol(h):
            SPb = 2 * (h % 2)
            pb = h % 2
            bufu = (h // 4) % 2
            wait("tensor", "VE", msk_ve[(h, KORD[15])])
            a_te = {}
            for sub in range(2):
                rm = sub
                for idx, (k, n0, n1) in enumerate(nmm):
                    fn = (lambda e, k_=k, r_=rm, n0_=n0, n1_=n1,
                          p_=SPb + sub, b_=pb: e.matmul(
                        PS[p_][:, n0_:n1_],
                        ckv[:, k_ * KVL + r_ * 128:k_ * KVL + r_ * 128 + 128],
                        probsT2[b_][:, OFFr[k_] + n0_ - QSr[k_]:
                                OFFr[k_] + n1_ - QSr[k_]],
                        start=(k_ == 0), stop=(idx == len(nmm) - 1)))
                    if idx == len(nmm) - 1:
                        a_te[sub] = emit("tensor", fn, "TE")
                    else:
                        emit("tensor", fn)
            evA = {}
            for sub in range(2):
                wait("scalar", "TE", a_te[sub])
                if h >= 1 and sub == 0:
                    wait("scalar", "TE", oh_te[h - 1])
                evA[sub] = emit("scalar", lambda e, s_=sub, p_=SPb + sub:
                                e.copy(olat_sb[:, s_ * NQ:(s_ + 1) * NQ],
                                       PS[p_][:]), "AC")
            b_te = {}
            for sub in range(2):
                rm = 2 + sub
                wait("tensor", "AC", evA[sub])
                for idx, (k, n0, n1) in enumerate(nmm):
                    fn = (lambda e, k_=k, r_=rm, n0_=n0, n1_=n1,
                          p_=SPb + sub, b_=pb: e.matmul(
                        PS[p_][:, n0_:n1_],
                        ckv[:, k_ * KVL + r_ * 128:k_ * KVL + r_ * 128 + 128],
                        probsT2[b_][:, OFFr[k_] + n0_ - QSr[k_]:
                                OFFr[k_] + n1_ - QSr[k_]],
                        start=(k_ == 0), stop=(idx == len(nmm) - 1)))
                    if idx == len(nmm) - 1:
                        b_te[sub] = emit("tensor", fn, "TE")
                    else:
                        emit("tensor", fn)
            evB = {}
            for sub in range(2):
                wait("scalar", "TE", b_te[sub])
                evB[sub] = emit("scalar", lambda e, s_=sub, p_=SPb + sub:
                                e.copy(olat_sb[:, (2 + s_) * NQ:
                                               (3 + s_) * NQ],
                                       PS[p_][:]), "AC")
            wait("tensor", "AC", evB[0])
            for idx, (k, n0, n1) in enumerate(nmm):
                fn = (lambda e, k_=k, n0_=n0, n1_=n1, b_=pb: e.matmul(
                    PS[SPb][:, n0_:n1_],
                    onesm_sb[:],
                    probsT2[b_][:, OFFr[k_] + n0_ - QSr[k_]:
                            OFFr[k_] + n1_ - QSr[k_]],
                    start=(k_ == 0), stop=(idx == len(nmm) - 1)))
                if idx == len(nmm) - 1:
                    dn_te[h] = emit("tensor", fn, "TE")
                else:
                    emit("tensor", fn)
            wait("scalar", "TE", dn_te[h])
            lw = emit("scalar", lambda e: e.activation(
                lnden[:], PS[SPb][:], AF.Ln), "AC")
            wait("scalar", "AC", lw)
            rec_ve[h] = emit("scalar", lambda e: e.activation(
                recip_sb[:], lnden[:], AF.Exp, scale=-1.0), "AC")
            wait("tensor", "AC", evB[1])
            wait("tensor", "GL", uv_gl[(h // 4) * 4])
            olat_col = {0: 0, 1: 2, 2: 1, 3: 3}
            for rm in range(4):
                oc = {0: 0, 1: 1, 2: 2, 3: 3}[rm]
                src_col = rm if rm < 2 else rm
                for nn2 in range(2):
                    fn = (lambda e, r_=rm, n_=nn2, h_=h, b_=bufu, c_=src_col:
                          e.matmul(
                        PS[SPb + 1][:, n_ * 512:(n_ + 1) * 512],
                        wuv_sb[b_][:, r_ * 512 + (h_ % 4) * 128:
                                   r_ * 512 + (h_ % 4) * 128 + 128],
                        olat_sb[:, c_ * NQ + n_ * 512:c_ * NQ + n_ * 512 + 512],
                        start=(r_ == 0), stop=(r_ == 3)))
                    if rm == 3 and nn2 == 1:
                        oh_te[h] = emit("tensor", fn, "TE")
                    else:
                        emit("tensor", fn)
            wait("vector", "TE", oh_te[h])
            wait("vector", "AC", rec_ve[h])
            at_ve[h] = emit("vector", lambda e, h_=h: e.tensor_tensor(
                attnT[:, h_ * NQ:(h_ + 1) * NQ], PS[SPb + 1][:], recip_sb[:],
                ALU.mult), "VE")

        emit_sc(0)
        emit_sc(1)
        for h in range(H):
            emit_ol(h)
            if h + 2 < H:
                emit_sc(h + 2)
        p3_ve = cnt["VE"]

        for n in ("probsT0", "probsT1", "mask_sb", "olat_sb", "recip_sb",
                  "qla3_0",
                  "qla3_1", "qro3_0", "qro3_1", "wuv_sb0", "wuv_sb1",
                  "ckvT", "ckv", "krope"):
            free(n)

        wait("gpsimd", "VE", p3_ve)
        wo_sb = [carve(f"wo_sb{i}", [128, 32 * 512], BF16) for i in range(2)]
        NPB = 8
        xp = [carve(f"xp_{i}", [128, 512], F32) for i in range(NPB)]
        gp = [carve(f"gp_{i}", [128, 512], F32) for i in range(NPB)]
        op_ = [carve(f"op_{i}", [128, 512], F32) for i in range(NPB)]

        wait("tensor", "VE", p3_ve)
        wait("sync", "VE", p3_ve)
        wait("scalar", "VE", p3_ve)
        wait("scalar", "SC", g_all_sc)
        wo_te = {}
        fin_ve = {}
        out_sy = {}
        xld = {}
        gld = {}
        last_out = None
        for og in range(8):
            bufw = og % 2
            if og >= 2:
                wait("gpsimd", "TE", wo_te[og - 2])
            ldw = gdma(wo_sb[bufw][:].rearrange("p (a n) -> p a n", a=32),
                       wout[0:H * DV, og * 512:(og + 1) * 512]
                       .rearrange("(a p) n -> p a n", p=128))
            wait("tensor", "GL", ldw)
            for j in range(8):
                sid = og * 8 + j
                slot = sid % 8
                pj = slot // 2
                pco = (slot % 2) * 512
                pb = sid % NPB
                if sid >= NPB:
                    wait("sync", "VE", fin_ve[sid - NPB])
                    wait("scalar", "VE", fin_ve[sid - NPB])
                xld[sid] = ydma(xp[pb][:],
                                xkv[j * 128:(j + 1) * 128,
                                    og * 512:(og + 1) * 512])
                gld[sid] = cdma(gp[pb][:],
                                g_scr[j * 128:(j + 1) * 128,
                                      og * 512:(og + 1) * 512])
                if sid >= NPB:
                    wait("tensor", "VE", fin_ve[sid - NPB])
                for hh2 in range(32):
                    fn = (lambda e, b_=bufw, h_=hh2, j_=j, p_=pj, o_=pco:
                          e.matmul(
                        PS[p_][:, o_:o_ + 512],
                        attnT[:, h_ * NQ + j_ * 128:h_ * NQ + (j_ + 1) * 128],
                        wo_sb[b_][:, h_ * 512:(h_ + 1) * 512],
                        start=(h_ == 0), stop=(h_ == 31)))
                    if hh2 == 31:
                        ts_ = emit("tensor", fn, "TE")
                    else:
                        emit("tensor", fn)
                if j == 7:
                    wo_te[og] = ts_
                wait("vector", "TE", ts_)
                wait("vector", "SY", xld[sid])
                wait("vector", "SC", gld[sid])
                if sid >= NPB:
                    wait("vector", "OUT", out_sy[sid - NPB])
                emit("vector", lambda e, b_=pb, p_=pj, o_=pco: e.tensor_tensor(
                    op_[b_][:], PS[p_][:, o_:o_ + 512], gp[b_][:], ALU.mult))
                fin_ve[sid] = emit("vector", lambda e, b_=pb: e.tensor_tensor(
                    op_[b_][:], op_[b_][:], xp[b_][:], ALU.add), "VE")
                wait("sync", "VE", fin_ve[sid])
                out_sy[sid] = emit("sync", lambda e, b_=pb, j_=j, o_=og:
                                   e.dma_start(
                    out[j_ * 128:(j_ + 1) * 128, o_ * 512:(o_ + 1) * 512],
                    op_[b_][:]), "OUT", 16)
                last_out = out_sy[sid]
        wait("sync", "OUT", last_out)

        @block.tensor
        def _(t):
            for fn in q["tensor"]:
                fn(t)

        @block.vector
        def _(v):
            for fn in q["vector"]:
                fn(v)

        @block.scalar
        def _(sc):
            for fn in q["scalar"]:
                fn(sc)

        @block.sync
        def _(sy):
            for fn in q["sync"]:
                fn(sy)

        @block.gpsimd
        def _(g):
            for fn in q["gpsimd"]:
                fn(g)

    return nc


def _rope_tables():
    BASE, FACTOR, BFAST, BSLOW, OLD_CTX = 10000.0, 40.0, 32.0, 1.0, 4096.0
    freqs = (1.0 / BASE ** (np.arange(0, DR, 2, dtype=np.float32) / DR)
             ).astype(np.float32)
    wavelengths = (2.0 * np.float32(np.pi) / freqs).astype(np.float32)
    ramp = np.clip((wavelengths / np.float32(OLD_CTX) - np.float32(BSLOW))
                   / (np.float32(BFAST) - np.float32(BSLOW)), 0.0, 1.0
                   ).astype(np.float32)
    scale = (1.0 - ramp + ramp * np.float32(FACTOR)).astype(np.float32)
    inv = (freqs / scale).astype(np.float32)
    pos = np.arange(S, dtype=np.float32)
    f = pos[:, None] * inv[None, :]
    return np.cos(f).astype(np.float32), np.sin(f).astype(np.float32)


_NC_CACHE = {}


def kernel(L0_attn_hidden_states, L0_attn_w_q_down, L0_attn_w_q_up_nope,
           L0_attn_w_q_up_rope, L0_attn_w_kv_down, L0_attn_w_k_rope,
           L0_attn_w_uk, L0_attn_w_uv, L0_attn_w_out, L0_ffn_W, L0_ffn_b,
           L0_ffn_Wq, L0_ffn_Wk, L0_ffn_A, L0_ffn_Bm):
    x = np.asarray(L0_attn_hidden_states, dtype=np.float32)
    cos_t, sin_t = _rope_tables()

    wuvT = np.ascontiguousarray(np.asarray(L0_attn_w_uv, np.float32).T)
    WTc = np.ascontiguousarray(np.asarray(L0_ffn_W, np.float32).T)
    WqT = np.ascontiguousarray(np.asarray(L0_ffn_Wq, np.float32).T)
    WkT = np.ascontiguousarray(np.asarray(L0_ffn_Wk, np.float32).T)
    ATc = np.ascontiguousarray(
        np.asarray(L0_ffn_A, np.float32).reshape(E * R, D).T)
    Bflc = np.ascontiguousarray(
        np.asarray(L0_ffn_Bm, np.float32).transpose(0, 2, 1).reshape(E * R, D))
    bvec = np.asarray(L0_ffn_b, np.float32).reshape(1, D)

    ident = np.eye(128, dtype=np.float32)
    onesm = np.ones((128, 128), np.float32)
    ones1 = np.ones((1, 128), np.float32)
    seg = np.zeros((E * DK, E), np.float32)
    for n in range(E * DK):
        seg[n, n // DK] = 1.0
    expm = np.zeros((E, E * R), np.float32)
    for er in range(E * R):
        expm[er // R, er] = 1.0

    tri = np.tril(np.ones((128, 128), np.float32), 0)
    tri = (np.arange(128)[:, None] <= np.arange(128)[None, :]).astype(
        np.float32)

    in_maps = []
    for core in range(8):
        b = core // 2
        par = core % 2
        pos = [2 * j + par for j in range(8)] + [2 * j + 1 - par
                                                 for j in range(8)]
        xkv = np.ascontiguousarray(
            x[b].reshape(16, 128, D)[pos].reshape(S, D))

        ck2c = np.empty((128, 16 * 64), np.float32)
        sk2c = np.empty((128, 16 * 64), np.float32)
        for k in range(16):
            p = pos[k]
            cb = cos_t[p * 128:(p + 1) * 128]
            sb_ = sin_t[p * 128:(p + 1) * 128]
            ck2c[:, k * 64:k * 64 + 32] = cb
            ck2c[:, k * 64 + 32:(k + 1) * 64] = cb
            sk2c[:, k * 64:k * 64 + 32] = sb_
            sk2c[:, k * 64 + 32:(k + 1) * 64] = sb_
        cq2c = np.empty((128, NQ), np.float32)
        sq2c = np.empty((128, NQ), np.float32)
        for j in range(8):
            p = pos[j]
            cb = cos_t[p * 128:(p + 1) * 128].T
            sb_ = sin_t[p * 128:(p + 1) * 128].T
            for rep in range(4):
                cq2c[rep * 32:(rep + 1) * 32, j * 128:(j + 1) * 128] = cb
                sq2c[rep * 32:(rep + 1) * 32, j * 128:(j + 1) * 128] = sb_

        maskc = np.empty((128, 16 * 128), np.float32)
        for k in range(16):
            if k < 8:
                maskc[:, k * 128:(k + 1) * 128] = tri
            else:
                maskc[:, k * 128:(k + 1) * 128] = 0.0 if par == 0 else 1.0

        in_maps.append({
            "xkv": xkv,
            "wqd": np.asarray(L0_attn_w_q_down, np.float32),
            "wqun": np.asarray(L0_attn_w_q_up_nope, np.float32),
            "wqur": np.asarray(L0_attn_w_q_up_rope, np.float32),
            "wkvd": np.asarray(L0_attn_w_kv_down, np.float32),
            "wkr": np.asarray(L0_attn_w_k_rope, np.float32),
            "wuk": np.asarray(L0_attn_w_uk, np.float32),
            "wuvT": wuvT,
            "wout": np.asarray(L0_attn_w_out, np.float32),
            "WTi": WTc,
            "bvec": bvec,
            "WqT": WqT,
            "WkT": WkT,
            "ATi": ATc,
            "Bfl": Bflc,
            "ck2": ck2c,
            "sk2": sk2c,
            "cq2": cq2c,
            "sq2": sq2c,
            "maski": maskc,
            "identi": ident,
            "onesmi": onesm,
            "ones1i": ones1,
            "segi": seg,
            "expmi": expm,
        })

    if "nc" not in _NC_CACHE:
        _NC_CACHE["nc"] = build_nc()
    nc = _NC_CACHE["nc"]
    res = run_bass_kernel_spmd(nc, in_maps, core_ids=list(range(8)))

    out_full = np.empty((B, S, D), np.float32)
    for core in range(8):
        b = core // 2
        par = core % 2
        oc = res.results[core]["out"]
        for j in range(8):
            p = 2 * j + par
            out_full[b, p * 128:(p + 1) * 128] = oc[j * 128:(j + 1) * 128]
    return out_full


# version 14
# speedup vs baseline: 1.0507x; 1.0507x over previous
import math
import numpy as np

import concourse.bass as bass
import concourse.mybir as mybir
from concourse.bass_utils import run_bass_kernel_spmd

F32 = mybir.dt.float32
BF16 = mybir.dt.bfloat16
AF = mybir.ActivationFunctionType
ALU = mybir.AluOpType

B, S, D = 4, 2048, 4096
NQ = 1024
H, DN, DR, DV, KVL, QL = 32, 128, 64, 128, 512, 1536
E, DK, R = 8, 32, 16
SCALE = 1.0 / math.sqrt(float(DN + DR))
SCALE_F = 1.0 / math.sqrt(float(DK))

QSr = [(k % 8) * 128 for k in range(16)]
NWr = [NQ - QSr[k] for k in range(16)]
OFFr = np.concatenate([[0], np.cumsum(NWr)]).astype(int).tolist()
PTOT = OFFr[16]

ENGS = ("tensor", "vector", "scalar", "sync", "gpsimd")


def _round32(x):
    return (x + 31) & ~31


class Arena:
    def __init__(self, cap):
        self.cap = cap
        self.blocks = {}
        self.free = [(0, cap)]
        self.hw = 0

    def alloc(self, name, size, side="left"):
        size = _round32(size)
        idxs = range(len(self.free)) if side == "left" else \
            range(len(self.free) - 1, -1, -1)
        for i in idxs:
            off, fsz = self.free[i]
            if fsz >= size:
                if side == "left":
                    self.blocks[name] = (off, size)
                    if fsz == size:
                        self.free.pop(i)
                    else:
                        self.free[i] = (off + size, fsz - size)
                else:
                    self.blocks[name] = (off + fsz - size, size)
                    if fsz == size:
                        self.free.pop(i)
                    else:
                        self.free[i] = (off, fsz - size)
                self.hw = max(self.hw, self.blocks[name][0] + size)
                return self.blocks[name][0]
        raise RuntimeError(f"SBUF OOM {name} ({size}B) free={self.free}")

    def alloc_at(self, name, off, size):
        size = _round32(size)
        for i, (o, s) in enumerate(self.free):
            if o <= off and off + size <= o + s:
                self.free.pop(i)
                if off > o:
                    self.free.append((o, off - o))
                if o + s > off + size:
                    self.free.append((off + size, o + s - off - size))
                self.free.sort()
                self.blocks[name] = (off, size)
                return off
        raise RuntimeError(f"SBUF alloc_at {name} @{off} unavailable")

    def release(self, name):
        off, size = self.blocks.pop(name)
        self.free.append((off, size))
        self.free.sort()
        merged = []
        for o, s in self.free:
            if merged and merged[-1][0] + merged[-1][1] == o:
                merged[-1][1] += s
            else:
                merged.append([o, s])
        self.free = [tuple(m) for m in merged]


def _nsplits(n0, n1):
    out = []
    a = n0
    while a < n1:
        b2 = min(n1, (a // 512 + 1) * 512)
        out.append((a, b2))
        a = b2
    return out


def build_nc():
    nc = bass.Bass("TRN2", target_bir_lowering=False)

    def din(name, shape, dtype=F32):
        return nc.dram_tensor(name, shape, dtype, kind="ExternalInput").ap()

    xkv = din("xkv", [S, D])
    wqd = din("wqd", [D, QL])
    wqun = din("wqun", [QL, H * DN])
    wqur = din("wqur", [QL, H * DR])
    wkvd = din("wkvd", [D, KVL])
    wkr = din("wkr", [D, DR])
    wuk = din("wuk", [H * DN, KVL])
    wuvT = din("wuvT", [KVL, H * DV])
    wout = din("wout", [H * DV, D])
    WTi = din("WTi", [D, D])
    bvec = din("bvec", [1, D])
    WqT = din("WqT", [D, E * DK])
    WkT = din("WkT", [D, E * DK])
    ATi = din("ATi", [D, E * R])
    Bfl = din("Bfl", [E * R, D])
    ck2 = din("ck2", [128, 16 * 64])
    sk2 = din("sk2", [128, 16 * 64])
    cq2 = din("cq2", [128, NQ])
    sq2 = din("sq2", [128, NQ])
    maski = din("maski", [128, 16 * 128])
    identi = din("identi", [128, 128])
    onesmi = din("onesmi", [128, 128])
    ones1i = din("ones1i", [1, 128])
    segi = din("segi", [E * DK, E])
    expmi = din("expmi", [E, E * R])

    out = nc.dram_tensor("out", [NQ, D], F32, kind="ExternalOutput").ap()

    ql_scr = nc.dram_tensor("ql_scr", [H * (KVL + DR), NQ], BF16).ap()
    wqd_bf = nc.dram_tensor("wqd_bf", [D, QL], BF16).ap()
    g_scr = nc.dram_tensor("g_scr", [NQ, D], F32).ap()

    q = {e: [] for e in ENGS}
    cnt = {}

    with (
        nc.semaphore("sTE") as _sTE,
        nc.semaphore("sVE") as _sVE,
        nc.semaphore("sAC") as _sAC,
        nc.semaphore("sSY") as _sSY,
        nc.semaphore("sSC") as _sSC,
        nc.semaphore("sGL") as _sGL,
        nc.semaphore("sOUT") as _sOUT,
        nc.semaphore("sPO") as _sPO,
        nc.sbuf_tensor("arena", [128, 204 * 1024], mybir.dt.uint8) as arena_t,
        nc.psum_tensor("ps0", [128, 1024], F32) as ps0,
        nc.psum_tensor("ps1", [128, 1024], F32) as ps1,
        nc.psum_tensor("ps2", [128, 1024], F32) as ps2,
        nc.psum_tensor("ps3", [128, 1024], F32) as ps3,
        nc.Block() as block,
    ):
        sems = {"TE": _sTE, "VE": _sVE, "AC": _sAC, "SY": _sSY, "SC": _sSC,
                "GL": _sGL, "OUT": _sOUT, "PO": _sPO}
        PS = [ps0, ps1, ps2, ps3]

        def inc(sem, k=1):
            cnt[sem] = cnt.get(sem, 0) + k
            return cnt[sem]

        def emit(eng, fn, sem=None, k=1):
            if sem is None:
                q[eng].append(fn)
                return None
            q[eng].append(lambda e, f=fn, s=sems[sem], kk=k: f(e).then_inc(s, kk))
            return inc(sem, k)

        def wait(eng, sem, val):
            if val and val > 0:
                q[eng].append(lambda e, s=sems[sem], v=val: e.wait_ge(s, v))

        arena_base = nc.lookup_mloc(arena_t).addr
        ar = Arena(204 * 1024)
        tens = {}

        def carve_at(name, shape, dtype, off):
            per_part = int(np.prod(shape[1:])) * mybir.dt.size(dtype)
            ar.alloc_at(name, off, per_part)
            h = nc.virtual_sbuf_tensor(name, shape, dtype)
            nc._tensor(h.name, h.shape, h.dtype, type="SB")
            mloc = nc.lookup_mloc(h)
            mloc.allocated = True
            mloc.addr = arena_base + off
            tens[name] = h
            return h

        def carve(name, shape, dtype, side="left"):
            per_part = int(np.prod(shape[1:])) * mybir.dt.size(dtype)
            ar.alloc(name, per_part, side=side)
            h = nc.virtual_sbuf_tensor(name, shape, dtype)
            nc._tensor(h.name, h.shape, h.dtype, type="SB")
            mloc = nc.lookup_mloc(h)
            mloc.allocated = True
            mloc.addr = arena_base + ar.blocks[name][0]
            tens[name] = h
            return h

        def free(name):
            ar.release(name)
            del tens[name]

        def gdma(dst, src):
            return emit("gpsimd", lambda e, d=dst, s=src: e.dma_start(d, s),
                        "GL", 16)

        def ydma(dst, src):
            return emit("sync", lambda e, d=dst, s=src: e.dma_start(d, s),
                        "SY", 16)

        def cdma(dst, src):
            return emit("scalar", lambda e, d=dst, s=src: e.dma_start(d, s),
                        "SC", 16)

        wkvd_sb = carve("wkvd_sb", [128, 32 * KVL], BF16)
        wkr_sb = carve("wkr_sb", [128, 32 * DR], BF16)
        ident_sb = carve("ident_sb", [128, 128], BF16, side="right")
        onesm_sb = carve("onesm_sb", [128, 128], BF16, side="right")
        ones1_sb = carve("ones1_sb", [1, 128], BF16, side="right")
        seg_sb = carve("seg_sb", [128, 2 * E], BF16, side="right")
        expm_sb = carve("expm_sb", [E, E * R], BF16, side="right")

        wkvd_ld = gdma(wkvd_sb[:].rearrange("p (a n) -> p a n", a=32),
                       wkvd.rearrange("(a p) n -> p a n", p=128))
        wkr_ld = gdma(wkr_sb[:].rearrange("p (a n) -> p a n", a=32),
                      wkr.rearrange("(a p) n -> p a n", p=128))
        ck2_sb = carve("ck2_sb", [128, 16 * 64], F32, side="right")
        sk2_sb = carve("sk2_sb", [128, 16 * 64], F32, side="right")
        gdma(ident_sb[:], identi[:, :])
        gdma(onesm_sb[:], onesmi[:, :])
        gdma(ones1_sb[:], ones1i[:, :])
        gdma(seg_sb[:].rearrange("p (a n) -> p a n", a=2),
             segi.rearrange("(a p) n -> p a n", p=128))
        consts_gl = gdma(expm_sb[:], expmi[:, :])
        ydma(ck2_sb[:], ck2[:, :])
        consts_sy = ydma(sk2_sb[:], sk2[:, :])

        xnO = carve("xnO", [128, 32 * NQ], BF16, side="right")
        sibT = [carve(f"sibT{i}", [128, 32 * 128], BF16) for i in range(2)]
        ckv = carve("ckv", [128, 16 * KVL], BF16, side="right")
        kraw = carve("kraw", [128, 16 * DR], F32)
        x_t = [carve(f"x_t{i}", [128, D], F32) for i in range(2)]
        xn_t = [carve(f"xn_t{i}", [128, D], BF16) for i in range(2)]
        stats = carve("stats", [128, 48], F32)
        mvt = carve("mvt", [128, 2], F32)
        nmean = carve("nmean", [128, 1], F32)
        veps = carve("veps", [128, 1], F32)
        sdv = carve("sdv", [128, 1], F32)
        rstd = carve("rstd", [128, 1], F32)

        tr_te = {}
        tr_ac = {}
        ckv_te = {}
        ckv_ac = {}
        gidx = 0
        wait("tensor", "GL", consts_gl)
        ln_rd = []
        for i in range(16):
            bufx = i % 2
            buf = i % 2
            own = i < 8
            if i >= 2:
                wait("sync", "PO", ln_rd[i - 2])
            ld = ydma(x_t[bufx][:], xkv[i * 128:(i + 1) * 128, :])
            wait("vector", "SY", ld)
            for c in range(7):
                emit("vector", lambda e, b_=bufx, c_=c: e.bn_stats(
                    stats[:, c_ * 6:(c_ + 1) * 6],
                    x_t[b_][:, c_ * 512:(c_ + 1) * 512]))
            w0 = emit("vector", lambda e, b_=bufx: e.bn_stats(
                stats[:, 42:48], x_t[b_][:, 3584:4096]), "VE")
            wait("vector", "VE", w0)
            w1 = emit("vector", lambda e: e.bn_aggr(mvt[:], stats[:]), "VE")
            wait("vector", "VE", w1)
            emit("vector", lambda e: e.tensor_scalar(
                nmean[:], mvt[:, 0:1], -1.0, None, ALU.mult))
            w2 = emit("vector", lambda e: e.tensor_scalar(
                veps[:], mvt[:, 1:2], 1e-5, None, ALU.add), "VE")
            wait("scalar", "VE", w2)
            w3 = emit("scalar", lambda e: e.activation(sdv[:], veps[:], AF.Sqrt),
                      "AC")
            wait("vector", "AC", w3)
            w4 = emit("vector", lambda e: e.reciprocal(rstd[:], sdv[:]), "VE")
            wait("vector", "VE", w4)
            if i >= 2:
                wait("gpsimd", "TE", tr_te[i - 2])
            wait("gpsimd", "VE", w4)
            w5 = emit("gpsimd", lambda e, b_=buf, bx_=bufx: e.tensor_scalar(
                xn_t[b_][:], x_t[bx_][:], nmean[:], rstd[:], ALU.add,
                ALU.mult), "PO")
            ln_rd.append(w5)
            wait("tensor", "PO", w5)
            if not own and i >= 10:
                wait("scalar", "TE", ckv_te[i - 2])
            for g in range(8):
                slot = gidx % 4
                pt = PS[slot // 2]
                coff = (slot % 2) * 512
                if gidx >= 4:
                    wait("tensor", "AC", tr_ac[gidx - 4])
                for c4 in range(4):
                    ic = g * 4 + c4
                    fn = (lambda e, b_=buf, ic_=ic, c_=c4, p_=pt,
                          o_=coff: e.matmul(
                        p_[:, o_ + c_ * 128:o_ + (c_ + 1) * 128],
                        xn_t[b_][:, ic_ * 128:(ic_ + 1) * 128],
                        ident_sb[:], start=(c_ == 0), stop=(c_ == 3),
                        skip_group_check=True))
                    if c4 == 3:
                        tg = emit("tensor", fn, "TE")
                    else:
                        emit("tensor", fn)
                if g == 7:
                    tr_te[i] = tg
                wait("scalar", "TE", tg)
                if own:
                    fn = (lambda e, g_=g, i_=i, p_=pt, o_=coff: e.copy(
                        xnO[:].rearrange("p (a n) -> p a n", a=32)
                        [:, g_ * 4:(g_ + 1) * 4, i_ * 128:(i_ + 1) * 128],
                        p_[:, o_:o_ + 512]
                        .rearrange("p (a n) -> p a n", a=4)))
                else:
                    fn = (lambda e, g_=g, b_=buf, p_=pt, o_=coff: e.copy(
                        sibT[b_][:, g_ * 512:(g_ + 1) * 512],
                        p_[:, o_:o_ + 512]))
                tr_ac[gidx] = emit("scalar", fn, "AC")
                gidx += 1
            if i == 0:
                wait("tensor", "GL", wkr_ld)
            wait("tensor", "AC", tr_ac[gidx - 1])
            if i >= 2:
                wait("tensor", "AC", ckv_ac[i - 2])
            par2 = i % 2
            for ic in range(32):
                if own:
                    rhs_t = xnO[:, ic * NQ + i * 128:ic * NQ + (i + 1) * 128]
                else:
                    rhs_t = sibT[buf][:, ic * 128:(ic + 1) * 128]
                emit("tensor", lambda e, r_=rhs_t, ic_=ic, p_=par2: e.matmul(
                    PS[2][:, p_ * 512:(p_ + 1) * 512],
                    r_,
                    wkvd_sb[:, ic_ * KVL:(ic_ + 1) * KVL],
                    start=(ic_ == 0), stop=(ic_ == 31)))
                fn = (lambda e, r_=rhs_t, ic_=ic, p_=par2: e.matmul(
                    PS[3][:, p_ * 512:p_ * 512 + 64],
                    r_,
                    wkr_sb[:, ic_ * DR:(ic_ + 1) * DR],
                    start=(ic_ == 0), stop=(ic_ == 31)))
                if ic == 31:
                    ckv_te[i] = emit("tensor", fn, "TE")
                else:
                    emit("tensor", fn)
            wait("scalar", "TE", ckv_te[i])
            emit("scalar", lambda e, i_=i, p_=par2: e.copy(
                ckv[:, i_ * KVL:(i_ + 1) * KVL],
                PS[2][:, p_ * 512:(p_ + 1) * 512]))
            ckv_ac[i] = emit("scalar", lambda e, i_=i, p_=par2: e.copy(
                kraw[:, i_ * DR:(i_ + 1) * DR],
                PS[3][:, p_ * 512:p_ * 512 + 64]), "AC")
        ckv_all_ac = cnt["AC"]

        p1offs = {n: ar.blocks[n][0] for n in ("x_t0", "x_t1", "xn_t0")}
        for n in ("x_t0", "x_t1", "xn_t0", "xn_t1", "stats", "mvt",
                  "nmean", "veps", "sdv", "rstd", "sibT0", "sibT1"):
            free(n)
        wait("gpsimd", "AC", ckv_all_ac)

        krN = carve("krN", [128, 16 * DR], BF16)
        krope = carve("krope", [64, S], BF16, side="right")
        ckvT = carve("ckvT", [128, 4 * S], BF16, side="right")
        vbuf = carve("vbuf", [128, 16 * DR], F32)

        kraw3 = kraw[:].rearrange("p (a d) -> p a d", a=16)
        vbuf3 = vbuf[:].rearrange("p (a d) -> p a d", a=16)
        krN3 = krN[:].rearrange("p (a d) -> p a d", a=16)
        wait("vector", "AC", ckv_all_ac)
        wait("vector", "SY", consts_sy)
        emit("vector", lambda e: e.tensor_tensor(vbuf[:], kraw[:], sk2_sb[:],
                                                 ALU.mult))
        w6 = emit("vector", lambda e: e.tensor_tensor(
            kraw[:], kraw[:], ck2_sb[:], ALU.mult), "VE")
        wait("vector", "VE", w6)
        emit("vector", lambda e: e.tensor_tensor(
            krN3[:, :, 0:32], kraw3[:, :, 0:32], vbuf3[:, :, 32:64],
            ALU.subtract))
        krN_ve = emit("vector", lambda e: e.tensor_tensor(
            krN3[:, :, 32:64], vbuf3[:, :, 0:32], kraw3[:, :, 32:64],
            ALU.add), "VE")

        wait("tensor", "VE", krN_ve)
        kt_ac = {}
        for i in range(16):
            pt2 = PS[i % 2]
            if i >= 2:
                wait("tensor", "AC", kt_ac[i - 2])
            fn = (lambda e, i_=i, p_=pt2: e.matmul(
                p_[0:64, 0:128],
                krN[:, i_ * DR:(i_ + 1) * DR],
                ident_sb[:], start=True, stop=True))
            ktd = emit("tensor", fn, "TE")
            wait("scalar", "TE", ktd)
            kt_ac[i] = emit("scalar", lambda e, i_=i, p_=pt2: e.copy(
                krope[0:64, i_ * 128:(i_ + 1) * 128], p_[0:64, 0:128]), "AC")

        krT_te = cnt["TE"]
        ck_ac = {}
        for tc in range(16):
            for rm in range(4):
                gi2 = tc * 4 + rm
                slot = gi2 % 4
                pt2 = PS[2 + slot // 2]
                coff = (slot % 2) * 512
                if gi2 >= 4:
                    wait("tensor", "AC", ck_ac[gi2 - 4])
                fn = (lambda e, t_=tc, r_=rm, p_=pt2, o_=coff: e.matmul(
                    p_[:, o_:o_ + 128],
                    ckv[:, t_ * KVL + r_ * 128:t_ * KVL + (r_ + 1) * 128],
                    ident_sb[:], start=True, stop=True))
                ckd = emit("tensor", fn, "TE")
                wait("scalar", "TE", ckd)
                ck_ac[gi2] = emit("scalar", lambda e, t_=tc, r_=rm, p_=pt2,
                                  o_=coff: e.copy(
                    ckvT[:, r_ * S + t_ * 128:r_ * S + (t_ + 1) * 128],
                    p_[:, o_:o_ + 128]), "AC")
        p1_ac = cnt["AC"]

        free("krN")
        free("kraw")
        free("vbuf")
        free("wkvd_sb")
        free("wkr_sb")
        free("ck2_sb")
        free("sk2_sb")
        xc_ve = None

        wqd_sb = [carve_at(f"wqd_sb{i}", [128, 4 * QL], BF16, p1offs[n])
                  for i, n in ((0, "x_t0"), (1, "x_t1"))]
        wait("gpsimd", "PO", ln_rd[15])
        wait("gpsimd", "VE", cnt["VE"])
        wait("gpsimd", "TE", krT_te)
        cqT_sb = carve("cqT_sb", [128, 12 * NQ], BF16)
        wqT_sb = carve("wqT_sb", [128, 32 * 256], BF16)
        wkT_sb = carve("wkT_sb", [128, 32 * 256], BF16)
        at_sb = carve("at_sb", [128, 32 * 128], BF16)

        wait("tensor", "AC", p1_ac)
        te_e8 = {}
        cq_ev = {}
        wb_sy = {}
        for grp in range(3):
            for e8 in range(8):
                bufq = e8 % 2
                prev = (grp, e8 - 2) if e8 >= 2 else \
                       ((grp - 1, 6 + e8) if grp >= 1 else None)
                if grp == 0:
                    if prev is not None:
                        wait("gpsimd", "TE", te_e8[prev])
                    if e8 >= 2:
                        wait("gpsimd", "SY", wb_sy[e8 - 2])
                    ld = gdma(wqd_sb[bufq][:]
                              .rearrange("p (a n) -> p a n", a=4),
                              wqd[(e8) * 512:(e8 + 1) * 512, :]
                              .rearrange("(a p) n -> p a n", p=128))
                    wait("tensor", "GL", ld)
                    wait("sync", "GL", ld)
                    wb_sy[e8] = ydma(
                        wqd_bf[e8 * 512:(e8 + 1) * 512, :]
                        .rearrange("(a p) n -> p a n", p=128),
                        wqd_sb[bufq][:].rearrange("p (a n) -> p a n", a=4))
                else:
                    if prev is not None:
                        wait("sync", "TE", te_e8[prev])
                    if grp == 1 and e8 < 2:
                        wait("sync", "SY", wb_sy[e8])
                    ld = ydma(wqd_sb[bufq][:]
                              .rearrange("p (a n) -> p a n", a=4),
                              wqd_bf[e8 * 512:(e8 + 1) * 512, :]
                              .rearrange("(a p) n -> p a n", p=128))
                    wait("tensor", "SY", ld)
                if grp >= 1 and e8 == 0:
                    wait("tensor", "VE", cq_ev[grp - 1])
                for i4 in range(4):
                    i = e8 * 4 + i4
                    for lg in range(4):
                        lm = grp * 4 + lg
                        for nn2 in range(2):
                            fn = (lambda e, b_=bufq, i4_=i4, i_=i, l_=lm,
                                  g_=lg, n_=nn2: e.matmul(
                                PS[g_][:, n_ * 512:(n_ + 1) * 512],
                                wqd_sb[b_][:, i4_ * QL + l_ * 128:
                                           i4_ * QL + l_ * 128 + 128],
                                xnO[:, i_ * NQ + n_ * 512:
                                    i_ * NQ + n_ * 512 + 512],
                                start=(i_ == 0), stop=(i_ == 31)))
                            if i4 == 3 and lg == 3 and nn2 == 1:
                                te_e8[(grp, e8)] = emit("tensor", fn, "TE")
                            else:
                                emit("tensor", fn)
                if grp == 0 and e8 >= 2:
                    pass
            wait("vector", "TE", te_e8[(grp, 7)])
            for lg in range(4):
                ev = emit("vector", lambda e, g_=lg, gr_=grp: e.tensor_copy(
                    cqT_sb[:, (gr_ * 4 + g_) * NQ:(gr_ * 4 + g_ + 1) * NQ],
                    PS[g_][:]),
                    "VE" if lg == 3 else None)
            cq_ev[grp] = ev
        cq_last_ve = cnt["VE"]

        free("wqd_sb0")
        free("wqd_sb1")

        qf_sb = carve("qf_sb", [128, NQ], F32)
        p_sb = carve("p_sb", [128, 2 * NQ], BF16)
        wf_sb = carve("wf_sb", [128, 64], BF16)
        wft_sb = carve("wft_sb", [8, NQ], BF16)
        hT_sb = carve("hT_sb", [128, NQ], F32)
        hwT_sb = carve("hwT_sb", [128, NQ], BF16, side="right")
        negmax = carve("negmax", [128, 1], F32)
        biasF = carve("biasF", [128, 1], F32)
        wfu_sb = carve("wfu_sb", [128, 8], F32)
        sumF = carve("sumF", [128, 1], F32)
        recF = carve("recF", [128, 1], F32)

        ldq = gdma(wqT_sb[:].rearrange("p (a n) -> p a n", a=32),
                   WqT.rearrange("(a p) n -> p a n", p=128))
        ldk = gdma(wkT_sb[:].rearrange("p (a n) -> p a n", a=32),
                   WkT.rearrange("(a p) n -> p a n", p=128))
        lda = gdma(at_sb[:].rearrange("p (a n) -> p a n", a=32),
                   ATi.rearrange("(a p) n -> p a n", p=128))

        wait("tensor", "GL", ldq)
        wait("tensor", "VE", cq_last_ve)
        pq_te = {}
        for nm in range(2):
            if nm == 1:
                wait("tensor", "VE", pq_te[0])
            for i in range(32):
                for nn2 in range(2):
                    fn = (lambda e, i_=i, m_=nm, n_=nn2: e.matmul(
                        PS[0][:, n_ * 512:(n_ + 1) * 512],
                        wqT_sb[:, i_ * 256 + m_ * 128:i_ * 256 + m_ * 128 + 128],
                        xnO[:, i_ * NQ + n_ * 512:i_ * NQ + n_ * 512 + 512],
                        start=(i_ == 0), stop=(i_ == 31)))
                    if i == 31 and nn2 == 1:
                        tq_ = emit("tensor", fn, "TE")
                    else:
                        emit("tensor", fn)
            wait("vector", "TE", tq_)
            evq = emit("vector", lambda e: e.tensor_copy(qf_sb[:], PS[0][:]),
                       "VE")
            if nm == 0:
                wait("tensor", "GL", ldk)
            wait("tensor", "VE", evq)
            for i in range(32):
                for nn2 in range(2):
                    fn = (lambda e, i_=i, m_=nm, n_=nn2: e.matmul(
                        PS[0][:, n_ * 512:(n_ + 1) * 512],
                        wkT_sb[:, i_ * 256 + m_ * 128:i_ * 256 + m_ * 128 + 128],
                        xnO[:, i_ * NQ + n_ * 512:i_ * NQ + n_ * 512 + 512],
                        start=(i_ == 0), stop=(i_ == 31)))
                    if i == 31 and nn2 == 1:
                        tk_ = emit("tensor", fn, "TE")
                    else:
                        emit("tensor", fn)
            wait("vector", "TE", tk_)
            pq_te[nm] = emit("vector", lambda e, m_=nm: e.tensor_tensor(
                p_sb[:, m_ * NQ:(m_ + 1) * NQ], PS[0][:], qf_sb[:], ALU.mult),
                "VE")

        wait("tensor", "VE", pq_te[1])
        exp_done_f = {}
        for qc in range(8):
            pti = 1 + (qc % 2)
            if qc >= 2:
                wait("tensor", "AC", exp_done_f[qc - 2])
            for nm in range(2):
                fn = (lambda e, m_=nm, q_=qc, p_=pti: e.matmul(
                    PS[p_][:, 0:8],
                    p_sb[:, m_ * NQ + q_ * 128:m_ * NQ + (q_ + 1) * 128],
                    seg_sb[:, m_ * 8:(m_ + 1) * 8],
                    start=(m_ == 0), stop=(m_ == 1)))
                if nm == 1:
                    tsf = emit("tensor", fn, "TE")
                else:
                    emit("tensor", fn)
            wait("vector", "TE", tsf)
            w7 = emit("vector", lambda e, p_=pti: e.tensor_reduce(
                negmax[:], PS[p_][:, 0:8], mybir.AxisListType.X, ALU.max,
                negate=True), "VE")
            wait("vector", "VE", w7)
            w8 = emit("vector", lambda e: e.tensor_scalar(
                biasF[:], negmax[:], SCALE_F, None, ALU.mult), "VE")
            wait("scalar", "VE", w8)
            exp_done_f[qc] = emit("scalar", lambda e, p_=pti: e.activation(
                wfu_sb[:], PS[p_][:, 0:8], AF.Exp, bias=biasF[:],
                scale=SCALE_F, accum_out=sumF[:]), "AC")
            wait("vector", "AC", exp_done_f[qc])
            w9 = emit("vector", lambda e: e.reciprocal(recF[:], sumF[:]), "VE")
            wait("vector", "VE", w9)
            emit("vector", lambda e, q_=qc: e.tensor_scalar(
                wf_sb[:, q_ * 8:(q_ + 1) * 8], wfu_sb[:], recF[:], None,
                ALU.mult), "VE")
        wf_all_ve = cnt["VE"]

        wait("tensor", "VE", wf_all_ve)
        for qc in range(8):
            fn = (lambda e, q_=qc: e.matmul(
                PS[3][0:8, q_ * 128:(q_ + 1) * 128],
                wf_sb[:, q_ * 8:(q_ + 1) * 8],
                ident_sb[:], start=(q_ % 4 == 0), stop=(q_ % 4 == 3),
                skip_group_check=True))
            if qc == 7:
                twf = emit("tensor", fn, "TE")
            else:
                emit("tensor", fn)
        wait("vector", "TE", twf)
        wft_ve = emit("vector", lambda e: e.tensor_copy(wft_sb[:], PS[3][0:8, :]),
                      "VE")

        wait("tensor", "VE", wft_ve)
        for nn2 in range(2):
            fn = (lambda e, n_=nn2: e.matmul(
                PS[3][:, n_ * 512:(n_ + 1) * 512],
                expm_sb[:],
                wft_sb[0:8, n_ * 512:(n_ + 1) * 512],
                start=True, stop=True))
            if nn2 == 1:
                twb = emit("tensor", fn, "TE")
            else:
                emit("tensor", fn)
        wait("tensor", "GL", lda)
        for i in range(32):
            for nn2 in range(2):
                fn = (lambda e, i_=i, n_=nn2: e.matmul(
                    PS[0][:, n_ * 512:(n_ + 1) * 512],
                    at_sb[:, i_ * 128:(i_ + 1) * 128],
                    xnO[:, i_ * NQ + n_ * 512:i_ * NQ + n_ * 512 + 512],
                    start=(i_ == 0), stop=(i_ == 31)))
                if i == 31 and nn2 == 1:
                    tht = emit("tensor", fn, "TE")
                else:
                    emit("tensor", fn)
        wait("vector", "TE", tht)
        hev = emit("vector", lambda e: e.tensor_copy(hT_sb[:], PS[0][:]), "VE")
        wait("vector", "TE", twb)
        wait("vector", "VE", hev)
        hw_ve = emit("vector", lambda e: e.tensor_tensor(
            hwT_sb[:], hT_sb[:], PS[3][:], ALU.mult), "VE")

        for n in ("wqT_sb", "wkT_sb", "at_sb", "qf_sb", "p_sb", "wf_sb",
                  "wft_sb", "hT_sb", "negmax", "biasF", "wfu_sb", "sumF",
                  "recF"):
            free(n)

        qla_sb = [carve(f"qla_sb{i}", [128, 4 * NQ], BF16) for i in range(2)]
        vtmp = carve("vtmp", [128, NQ], F32)
        cq2_sb = carve("cq2_sb", [128, NQ], F32)
        sq2_sb = carve("sq2_sb", [128, NQ], F32)
        wqun_sb = [carve(f"wqun_sb{i}", [128, 12 * 256], BF16) for i in range(4)]
        wqur_sb = [carve(f"wqur_sb{i}", [128, 12 * 256], BF16) for i in range(2)]
        wuk_sb = [carve(f"wuk_sb{i}", [128, 2 * 512], BF16) for i in range(2)]
        qno_sb = [carve(f"qno_sb{i}", [128, NQ], BF16) for i in range(2)]
        qro_sb = carve("qro_sb", [128, NQ], BF16)

        wait("sync", "VE", hw_ve)
        wait("gpsimd", "VE", hw_ve)
        ydma(cq2_sb[:], cq2[:, :])
        ld_tab = ydma(sq2_sb[:], sq2[:, :])

        wait("tensor", "VE", hw_ve)

        nld = {}
        rld = {}
        kld = {}
        qn_te = {}
        qr_te = {}
        uk_te = {}
        scr_sy = {}
        rope_ve = {}
        la_ve = {}
        la_prev = {}
        la_cur = {}
        evn_b = {}
        qno_last = {}
        qla_sy = {}
        for hp in range(16):
            h0, h1 = 2 * hp, 2 * hp + 1
            bufp = hp % 4
            if hp >= 4:
                wait("gpsimd", "TE", qn_te[hp - 4])
            nld[hp] = gdma(wqun_sb[bufp][:].rearrange("p (a n) -> p a n", a=12),
                           wqun[0:QL, hp * 256:(hp + 1) * 256]
                           .rearrange("(a p) n -> p a n", p=128))
            if hp >= 2:
                wait("gpsimd", "TE", uk_te[hp - 2])
            kld[hp] = gdma(wuk_sb[hp % 2][:]
                           .rearrange("p (a n) -> p a n", a=2),
                           wuk[hp * 256:(hp + 1) * 256, :]
                           .rearrange("(a p) n -> p a n", p=128))
            if hp % 2 == 0:
                bufr = (hp // 2) % 2
                if hp >= 4:
                    wait("gpsimd", "TE", qr_te[hp - 3])
                rld[hp] = gdma(wqur_sb[bufr][:]
                               .rearrange("p (a n) -> p a n", a=12),
                               wqur[0:QL, hp * 128:(hp + 2) * 128]
                               .rearrange("(a p) n -> p a n", p=128))


            rbase = (hp // 2) * 2
            bufr = (hp // 2) % 2
            rp = hp % 2
            wait("tensor", "GL", rld[rbase])
            if hp >= 2:
                wait("tensor", "VE", rope_ve[hp - 2])
            for li in range(12):
                for nn2 in range(2):
                    fn = (lambda e, l_=li, n_=nn2, p_=hp, b_=bufr, r_=rp:
                          e.matmul(
                        PS[r_][:, n_ * 512:(n_ + 1) * 512],
                        wqur_sb[b_][:, l_ * 256 + (p_ % 2) * 128:
                                    l_ * 256 + (p_ % 2) * 128 + 128],
                        cqT_sb[:, l_ * NQ + n_ * 512:l_ * NQ + n_ * 512 + 512],
                        start=(l_ == 0), stop=(l_ == 11)))
                    if li == 11 and nn2 == 1:
                        qr_te[hp] = emit("tensor", fn, "TE")
                    else:
                        emit("tensor", fn)
            if hp == 0:
                wait("vector", "SY", ld_tab)
            wait("vector", "TE", qr_te[hp])
            if hp >= 1:
                wait("vector", "SY", scr_sy[hp - 1])
            emit("vector", lambda e, r_=rp: e.tensor_tensor(
                vtmp[:], PS[r_][:], sq2_sb[:], ALU.mult))
            w10 = emit("vector", lambda e, r_=rp: e.tensor_tensor(
                PS[r_][:], PS[r_][:], cq2_sb[:], ALU.mult), "VE")
            wait("vector", "VE", w10)
            for bb in range(2):
                o = bb * 64
                emit("vector", lambda e, o_=o, r_=rp: e.tensor_tensor(
                    qro_sb[o_:o_ + 32, :], PS[r_][o_:o_ + 32, :],
                    vtmp[o_ + 32:o_ + 64, :], ALU.subtract))
                if bb == 1:
                    rope_ve[hp] = emit("vector", lambda e, o_=o, r_=rp:
                                       e.tensor_tensor(
                        qro_sb[o_ + 32:o_ + 64, :], vtmp[o_:o_ + 32, :],
                        PS[r_][o_ + 32:o_ + 64, :], ALU.add), "VE")
                else:
                    emit("vector", lambda e, o_=o, r_=rp: e.tensor_tensor(
                        qro_sb[o_ + 32:o_ + 64, :], vtmp[o_:o_ + 32, :],
                        PS[r_][o_ + 32:o_ + 64, :], ALU.add))

            for bb, hh2 in ((0, h0), (1, h1)):
                pq = 2 + bb
                wait("tensor", "GL", nld[hp])
                if la_prev.get(pq):
                    wait("tensor", "AC", la_prev[pq])
                for li in range(12):
                    for nn2 in range(2):
                        fn = (lambda e, l_=li, n_=nn2, b_=bufp, c_=bb,
                              p_=pq: e.matmul(
                            PS[p_][:, n_ * 512:(n_ + 1) * 512],
                            wqun_sb[b_][:, l_ * 256 + c_ * 128:
                                        l_ * 256 + c_ * 128 + 128],
                            cqT_sb[:, l_ * NQ + n_ * 512:
                                   l_ * NQ + n_ * 512 + 512],
                            start=(l_ == 0), stop=(l_ == 11)))
                        if li == 11 and nn2 == 1:
                            tqn = emit("tensor", fn, "TE")
                        else:
                            emit("tensor", fn)
                if bb == 1:
                    qn_te[hp] = tqn
                wait("scalar", "TE", tqn)
                if qno_last.get(bb):
                    wait("scalar", "TE", qno_last[bb])
                evn_b[bb] = emit("scalar", lambda e, c_=bb, p_=pq: e.copy(
                    qno_sb[c_][:], PS[p_][:]), "AC")
            bufk = hp % 2
            wait("tensor", "GL", kld[hp])
            for bb, hh2 in ((0, h0), (1, h1)):
                wait("tensor", "AC", evn_b[bb])
                for rm in range(4):
                    pr = 2 + ((rm + bb) % 2)
                    if rm >= 2:
                        wait("tensor", "AC", la_cur[(bb, rm - 2)])
                    elif bb == 0 and rm == 1:
                        wait("tensor", "AC", evn_b[1])
                    elif bb == 1 and rm == 0:
                        wait("tensor", "AC", la_cur[(0, 3)])
                    elif bb == 1 and rm == 1:
                        wait("tensor", "AC", la_cur[(0, 2)])
                    for nn2 in range(2):
                        fn = (lambda e, r_=rm, n_=nn2, b_=bufk, h_=hh2,
                              p_=pr, c_=bb: e.matmul(
                            PS[p_][:, n_ * 512:(n_ + 1) * 512],
                            wuk_sb[b_][:, (h_ % 2) * 512 + r_ * 128:
                                       (h_ % 2) * 512 + r_ * 128 + 128],
                            qno_sb[c_][:, n_ * 512:n_ * 512 + 512],
                            start=True, stop=True))
                        if nn2 == 1:
                            tla = emit("tensor", fn, "TE")
                        else:
                            emit("tensor", fn)
                    if rm == 3:
                        qno_last[bb] = tla
                    if bb == 1 and rm == 3:
                        uk_te[hp] = tla
                    wait("scalar", "TE", tla)
                    if rm == 0 and qla_sy.get(bb):
                        wait("scalar", "SY", qla_sy[bb])
                    la_cur[(bb, rm)] = emit(
                        "scalar", lambda e, r_=rm, p_=pr, c_=bb: e.copy(
                            qla_sb[c_][:, r_ * NQ:(r_ + 1) * NQ],
                            PS[p_][:]), "AC")
                if bb == 1:
                    la_prev[2] = la_cur[(1, 3)]
                    la_prev[3] = la_cur[(1, 2)]
                wait("sync", "AC", la_cur[(bb, 3)])
                wait("sync", "AC", la_cur[(bb, 2)])
                qla_sy[bb] = ydma(
                    ql_scr[hh2 * 576:hh2 * 576 + 512, :]
                    .rearrange("(a p) n -> p a n", p=128),
                    qla_sb[bb][:].rearrange("p (a n) -> p a n", a=4))
                wait("sync", "VE", rope_ve[hp])
                scr_sy[hp] = ydma(ql_scr[hh2 * 576 + 512:hh2 * 576 + 576, :],
                                  qro_sb[bb * 64:(bb + 1) * 64, :])
        p2b_sy = cnt["SY"]

        p2b_ac = cnt["AC"]
        free("cqT_sb")
        for n in ("cq2_sb", "sq2_sb", "wqun_sb0", "wqun_sb1",
                  "wqun_sb2", "wqun_sb3",
                  "wqur_sb0", "wqur_sb1", "wuk_sb0", "wuk_sb1", "qno_sb0",
                  "qno_sb1", "qla_sb0", "qla_sb1", "qro_sb", "vtmp"):
            free(n)

        wait("gpsimd", "SY", p2b_sy)
        wait("scalar", "SY", p2b_sy)
        bfl_sb = carve("bfl_sb", [128, D], BF16)
        b_sb = carve("b_sb", [1, D], BF16)
        wt_sb = [carve(f"wt_sb{i}", [128, 32 * 512], BF16) for i in range(2)]
        g_t = [carve(f"g_t{i}", [128, 512], F32) for i in range(2)]

        ldb1 = gdma(bfl_sb[:], Bfl[:, :])
        ldb2 = gdma(b_sb[:], bvec[:, :])
        wait("tensor", "GL", ldb2)
        wait("tensor", "VE", hw_ve)
        wait("tensor", "VE", rope_ve[15])
        wait("tensor", "AC", p2b_ac)

        wt_te = {}
        sil_ac = {}
        gdma_sc = {}
        for og in range(8):
            bufw = og % 2
            if og >= 2:
                wait("gpsimd", "TE", wt_te[og - 2])
            ldw = gdma(wt_sb[bufw][:].rearrange("p (a n) -> p a n", a=32),
                       WTi[0:D, og * 512:(og + 1) * 512]
                       .rearrange("(a p) n -> p a n", p=128))
            wait("tensor", "GL", ldw)
            for j in range(8):
                pj = j % 4
                if og * 8 + j >= 4:
                    prev_id = og * 8 + j - 4
                    wait("tensor", "AC", sil_ac[prev_id])
                for i in range(32):
                    emit("tensor", lambda e, b_=bufw, i_=i, j_=j, p_=pj:
                         e.matmul(
                        PS[p_][:, 0:512],
                        xnO[:, i_ * NQ + j_ * 128:i_ * NQ + (j_ + 1) * 128],
                        wt_sb[b_][:, i_ * 512:(i_ + 1) * 512],
                        start=(i_ == 0), stop=False))
                emit("tensor", lambda e, o_=og, p_=pj: e.matmul(
                    PS[p_][:, 0:512],
                    ones1_sb[:],
                    b_sb[0:1, o_ * 512:(o_ + 1) * 512],
                    start=False, stop=False))
                fn = (lambda e, o_=og, j_=j, p_=pj: e.matmul(
                    PS[p_][:, 0:512],
                    hwT_sb[:, j_ * 128:(j_ + 1) * 128],
                    bfl_sb[:, o_ * 512:(o_ + 1) * 512],
                    start=False, stop=True))
                tid = emit("tensor", fn, "TE")
                if j == 7:
                    wt_te[og] = tid
                wait("scalar", "TE", tid)
                gid = og * 8 + j
                if gid >= 2:
                    wait("scalar", "SC", gdma_sc[gid - 2])
                sil_ac[gid] = emit("scalar", lambda e, b_=gid % 2, p_=pj:
                                   e.activation(g_t[b_][:], PS[p_][:, 0:512],
                                                AF.Silu), "AC")
                wait("scalar", "AC", sil_ac[gid])
                gdma_sc[gid] = cdma(g_scr[j * 128:(j + 1) * 128,
                                          og * 512:(og + 1) * 512],
                                    g_t[gid % 2][:])
        g_all_sc = cnt["SC"]

        free("wt_sb0")
        free("wt_sb1")
        free("g_t0")
        free("g_t1")
        free("bfl_sb")
        free("b_sb")
        free("hwT_sb")
        free("xnO")

        wait("gpsimd", "SC", g_all_sc)
        attnT = carve("attnT", [128, H * NQ], BF16, side="right")
        probsT2 = [carve(f"probsT{i}", [128, PTOT], BF16, side="right")
                   for i in range(2)]
        mask_sb = carve("mask_sb", [128, 16 * 128], BF16, side="right")
        olat_sb = carve("olat_sb", [128, 4 * NQ], BF16)
        recip_sb = carve("recip_sb", [128, NQ], F32)
        lnden = carve("lnden", [128, NQ], F32)
        qla3 = [carve(f"qla3_{i}", [128, 4 * NQ], BF16) for i in range(2)]
        qro3 = [carve(f"qro3_{i}", [64, NQ], BF16) for i in range(2)]
        wuv_sb = [carve(f"wuv_sb{i}", [128, 4 * 512], BF16) for i in range(2)]

        ld_mask = gdma(mask_sb[:], maski[:, :])
        wait("sync", "SC", g_all_sc)

        KORD = [0, 8, 1, 9, 2, 10, 3, 11, 4, 12, 5, 13, 6, 14, 7, 15]
        nmm = []
        for k in range(16):
            for (n0, n1) in _nsplits(QSr[k], NQ):
                nmm.append((k, n0, n1))

        ql_sy = {}
        uv_gl = {}
        sc_te = {}
        exp_ac = {}
        msk_ve = {}
        dn_te = {}
        rec_ve = {}
        oh_te = {}
        at_ve = {}

        def emit_sc(h):
            bufh = h % 2
            SPb = 2 * (h % 2)
            pb = h % 2
            if h >= 2:
                wait("sync", "TE", sc_te[(h - 2, KORD[15])])
            ydma(qla3[bufh][:].rearrange("p (a n) -> p a n", a=4),
                 ql_scr[h * 576:h * 576 + 512, :]
                 .rearrange("(a p) n -> p a n", p=128))
            ql_sy[h] = ydma(qro3[bufh][:],
                            ql_scr[h * 576 + 512:h * 576 + 576, :])
            if h % 4 == 0:
                bufu = (h // 4) % 2
                if h >= 8:
                    wait("gpsimd", "TE", oh_te[h - 5])
                uv_gl[h] = gdma(wuv_sb[bufu][:]
                                .rearrange("p (a n) -> p a n", a=4),
                                wuvT[0:KVL, h * 128:(h + 4) * 128]
                                .rearrange("(a p) n -> p a n", p=128))
            wait("tensor", "SY", ql_sy[h])
            if h == 0:
                wait("tensor", "GL", ld_mask)
                wait("tensor", "AC", sil_ac[63])
            elif h == 1:
                pass
            else:
                pass
            for j, k in enumerate(KORD):
                pk = SPb + (j % 2)
                if h >= 2 and j == 0:
                    wait("tensor", "AC", rec_ve[h - 2])
                if h >= 2 and j == 1:
                    wait("tensor", "VE", at_ve[h - 2])
                if j >= 2:
                    wait("tensor", "AC", exp_ac[(h, KORD[j - 2])])
                mmlist = []
                for rm in range(4):
                    for (n0, n1) in _nsplits(QSr[k], NQ):
                        mmlist.append((rm, n0, n1, False))
                for (n0, n1) in _nsplits(QSr[k], NQ):
                    mmlist.append((4, n0, n1, True))
                for idx, (rm, n0, n1, isrope) in enumerate(mmlist):
                    last = idx == len(mmlist) - 1
                    if isrope:
                        fn = (lambda e, k_=k, n0_=n0, n1_=n1, p_=pk, b_=bufh:
                              e.matmul(
                            PS[p_][0:128, n0_:n1_],
                            krope[:, k_ * 128:(k_ + 1) * 128],
                            qro3[b_][0:64, n0_:n1_],
                            start=False, stop=True))
                    else:
                        fn = (lambda e, k_=k, r_=rm, n0_=n0, n1_=n1, p_=pk,
                              b_=bufh: e.matmul(
                            PS[p_][:, n0_:n1_],
                            ckvT[:, r_ * S + k_ * 128:r_ * S + (k_ + 1) * 128],
                            qla3[b_][:, r_ * NQ + n0_:r_ * NQ + n1_],
                            start=(r_ == 0), stop=False))
                    if last:
                        sc_te[(h, k)] = emit("tensor", fn, "TE")
                    else:
                        emit("tensor", fn)
                wait("scalar", "TE", sc_te[(h, k)])
                if j == 0 and h >= 2:
                    wait("scalar", "TE", dn_te[h - 2])
                exp_ac[(h, k)] = emit("scalar", lambda e, k_=k, p_=pk, b_=pb:
                                      e.activation(
                    probsT2[b_][:, OFFr[k_]:OFFr[k_] + NWr[k_]],
                    PS[p_][:, QSr[k_]:NQ], AF.Exp, scale=SCALE), "AC")
                wait("vector", "AC", exp_ac[(h, k)])
                msk_ve[(h, k)] = emit("vector", lambda e, k_=k, b_=pb:
                                      e.tensor_tensor(
                    probsT2[b_][:, OFFr[k_]:OFFr[k_] + 128],
                    probsT2[b_][:, OFFr[k_]:OFFr[k_] + 128],
                    mask_sb[:, k_ * 128:(k_ + 1) * 128], ALU.mult), "VE")

        def emit_ol(h):
            SPb = 2 * (h % 2)
            pb = h % 2
            bufu = (h // 4) % 2
            wait("tensor", "VE", msk_ve[(h, KORD[15])])
            a_te = {}
            for sub in range(2):
                rm = sub
                for idx, (k, n0, n1) in enumerate(nmm):
                    fn = (lambda e, k_=k, r_=rm, n0_=n0, n1_=n1,
                          p_=SPb + sub, b_=pb: e.matmul(
                        PS[p_][:, n0_:n1_],
                        ckv[:, k_ * KVL + r_ * 128:k_ * KVL + r_ * 128 + 128],
                        probsT2[b_][:, OFFr[k_] + n0_ - QSr[k_]:
                                OFFr[k_] + n1_ - QSr[k_]],
                        start=(k_ == 0), stop=(idx == len(nmm) - 1)))
                    if idx == len(nmm) - 1:
                        a_te[sub] = emit("tensor", fn, "TE")
                    else:
                        emit("tensor", fn)
            evA = {}
            for sub in range(2):
                wait("scalar", "TE", a_te[sub])
                if h >= 1 and sub == 0:
                    wait("scalar", "TE", oh_te[h - 1])
                evA[sub] = emit("scalar", lambda e, s_=sub, p_=SPb + sub:
                                e.copy(olat_sb[:, s_ * NQ:(s_ + 1) * NQ],
                                       PS[p_][:]), "AC")
            b_te = {}
            for sub in range(2):
                rm = 2 + sub
                wait("tensor", "AC", evA[sub])
                for idx, (k, n0, n1) in enumerate(nmm):
                    fn = (lambda e, k_=k, r_=rm, n0_=n0, n1_=n1,
                          p_=SPb + sub, b_=pb: e.matmul(
                        PS[p_][:, n0_:n1_],
                        ckv[:, k_ * KVL + r_ * 128:k_ * KVL + r_ * 128 + 128],
                        probsT2[b_][:, OFFr[k_] + n0_ - QSr[k_]:
                                OFFr[k_] + n1_ - QSr[k_]],
                        start=(k_ == 0), stop=(idx == len(nmm) - 1)))
                    if idx == len(nmm) - 1:
                        b_te[sub] = emit("tensor", fn, "TE")
                    else:
                        emit("tensor", fn)
            evB = {}
            for sub in range(2):
                wait("scalar", "TE", b_te[sub])
                evB[sub] = emit("scalar", lambda e, s_=sub, p_=SPb + sub:
                                e.copy(olat_sb[:, (2 + s_) * NQ:
                                               (3 + s_) * NQ],
                                       PS[p_][:]), "AC")
            wait("tensor", "AC", evB[0])
            for idx, (k, n0, n1) in enumerate(nmm):
                fn = (lambda e, k_=k, n0_=n0, n1_=n1, b_=pb: e.matmul(
                    PS[SPb][:, n0_:n1_],
                    onesm_sb[:],
                    probsT2[b_][:, OFFr[k_] + n0_ - QSr[k_]:
                            OFFr[k_] + n1_ - QSr[k_]],
                    start=(k_ == 0), stop=(idx == len(nmm) - 1)))
                if idx == len(nmm) - 1:
                    dn_te[h] = emit("tensor", fn, "TE")
                else:
                    emit("tensor", fn)
            wait("scalar", "TE", dn_te[h])
            lw = emit("scalar", lambda e: e.activation(
                lnden[:], PS[SPb][:], AF.Ln), "AC")
            wait("scalar", "AC", lw)
            rec_ve[h] = emit("scalar", lambda e: e.activation(
                recip_sb[:], lnden[:], AF.Exp, scale=-1.0), "AC")
            wait("tensor", "AC", evB[1])
            wait("tensor", "GL", uv_gl[(h // 4) * 4])
            olat_col = {0: 0, 1: 2, 2: 1, 3: 3}
            for rm in range(4):
                oc = {0: 0, 1: 1, 2: 2, 3: 3}[rm]
                src_col = rm if rm < 2 else rm
                for nn2 in range(2):
                    fn = (lambda e, r_=rm, n_=nn2, h_=h, b_=bufu, c_=src_col:
                          e.matmul(
                        PS[SPb + 1][:, n_ * 512:(n_ + 1) * 512],
                        wuv_sb[b_][:, r_ * 512 + (h_ % 4) * 128:
                                   r_ * 512 + (h_ % 4) * 128 + 128],
                        olat_sb[:, c_ * NQ + n_ * 512:c_ * NQ + n_ * 512 + 512],
                        start=(r_ == 0), stop=(r_ == 3)))
                    if rm == 3 and nn2 == 1:
                        oh_te[h] = emit("tensor", fn, "TE")
                    else:
                        emit("tensor", fn)
            wait("vector", "TE", oh_te[h])
            wait("vector", "AC", rec_ve[h])
            at_ve[h] = emit("vector", lambda e, h_=h: e.tensor_tensor(
                attnT[:, h_ * NQ:(h_ + 1) * NQ], PS[SPb + 1][:], recip_sb[:],
                ALU.mult), "VE")

        emit_sc(0)
        emit_sc(1)
        for h in range(H):
            emit_ol(h)
            if h + 2 < H:
                emit_sc(h + 2)
        p3_ve = cnt["VE"]

        for n in ("probsT0", "probsT1", "mask_sb", "olat_sb", "recip_sb",
                  "qla3_0",
                  "qla3_1", "qro3_0", "qro3_1", "wuv_sb0", "wuv_sb1",
                  "ckvT", "ckv", "krope"):
            free(n)

        wait("gpsimd", "VE", p3_ve)
        wo_sb = [carve(f"wo_sb{i}", [128, 32 * 512], BF16) for i in range(2)]
        NPB = 8
        xp = [carve(f"xp_{i}", [128, 512], F32) for i in range(NPB)]
        gp = [carve(f"gp_{i}", [128, 512], F32) for i in range(NPB)]
        op_ = [carve(f"op_{i}", [128, 512], F32) for i in range(NPB)]

        wait("tensor", "VE", p3_ve)
        wait("sync", "VE", p3_ve)
        wait("scalar", "VE", p3_ve)
        wait("scalar", "SC", g_all_sc)
        wo_te = {}
        fin_ve = {}
        out_sy = {}
        xld = {}
        gld = {}
        last_out = None
        for og in range(8):
            bufw = og % 2
            if og >= 2:
                wait("gpsimd", "TE", wo_te[og - 2])
            ldw = gdma(wo_sb[bufw][:].rearrange("p (a n) -> p a n", a=32),
                       wout[0:H * DV, og * 512:(og + 1) * 512]
                       .rearrange("(a p) n -> p a n", p=128))
            wait("tensor", "GL", ldw)
            for j in range(8):
                sid = og * 8 + j
                slot = sid % 8
                pj = slot // 2
                pco = (slot % 2) * 512
                pb = sid % NPB
                if sid >= NPB:
                    wait("sync", "VE", fin_ve[sid - NPB])
                    wait("scalar", "VE", fin_ve[sid - NPB])
                xld[sid] = ydma(xp[pb][:],
                                xkv[j * 128:(j + 1) * 128,
                                    og * 512:(og + 1) * 512])
                gld[sid] = cdma(gp[pb][:],
                                g_scr[j * 128:(j + 1) * 128,
                                      og * 512:(og + 1) * 512])
                if sid >= NPB:
                    wait("tensor", "VE", fin_ve[sid - NPB])
                for hh2 in range(32):
                    fn = (lambda e, b_=bufw, h_=hh2, j_=j, p_=pj, o_=pco:
                          e.matmul(
                        PS[p_][:, o_:o_ + 512],
                        attnT[:, h_ * NQ + j_ * 128:h_ * NQ + (j_ + 1) * 128],
                        wo_sb[b_][:, h_ * 512:(h_ + 1) * 512],
                        start=(h_ == 0), stop=(h_ == 31)))
                    if hh2 == 31:
                        ts_ = emit("tensor", fn, "TE")
                    else:
                        emit("tensor", fn)
                if j == 7:
                    wo_te[og] = ts_
                wait("vector", "TE", ts_)
                wait("vector", "SY", xld[sid])
                wait("vector", "SC", gld[sid])
                if sid >= NPB:
                    wait("vector", "OUT", out_sy[sid - NPB])
                emit("vector", lambda e, b_=pb, p_=pj, o_=pco: e.tensor_tensor(
                    op_[b_][:], PS[p_][:, o_:o_ + 512], gp[b_][:], ALU.mult))
                fin_ve[sid] = emit("vector", lambda e, b_=pb: e.tensor_tensor(
                    op_[b_][:], op_[b_][:], xp[b_][:], ALU.add), "VE")
                wait("sync", "VE", fin_ve[sid])
                out_sy[sid] = emit("sync", lambda e, b_=pb, j_=j, o_=og:
                                   e.dma_start(
                    out[j_ * 128:(j_ + 1) * 128, o_ * 512:(o_ + 1) * 512],
                    op_[b_][:]), "OUT", 16)
                last_out = out_sy[sid]
        wait("sync", "OUT", last_out)

        @block.tensor
        def _(t):
            for fn in q["tensor"]:
                fn(t)

        @block.vector
        def _(v):
            for fn in q["vector"]:
                fn(v)

        @block.scalar
        def _(sc):
            for fn in q["scalar"]:
                fn(sc)

        @block.sync
        def _(sy):
            for fn in q["sync"]:
                fn(sy)

        @block.gpsimd
        def _(g):
            for fn in q["gpsimd"]:
                fn(g)

    return nc


def _rope_tables():
    BASE, FACTOR, BFAST, BSLOW, OLD_CTX = 10000.0, 40.0, 32.0, 1.0, 4096.0
    freqs = (1.0 / BASE ** (np.arange(0, DR, 2, dtype=np.float32) / DR)
             ).astype(np.float32)
    wavelengths = (2.0 * np.float32(np.pi) / freqs).astype(np.float32)
    ramp = np.clip((wavelengths / np.float32(OLD_CTX) - np.float32(BSLOW))
                   / (np.float32(BFAST) - np.float32(BSLOW)), 0.0, 1.0
                   ).astype(np.float32)
    scale = (1.0 - ramp + ramp * np.float32(FACTOR)).astype(np.float32)
    inv = (freqs / scale).astype(np.float32)
    pos = np.arange(S, dtype=np.float32)
    f = pos[:, None] * inv[None, :]
    return np.cos(f).astype(np.float32), np.sin(f).astype(np.float32)


_NC_CACHE = {}


def kernel(L0_attn_hidden_states, L0_attn_w_q_down, L0_attn_w_q_up_nope,
           L0_attn_w_q_up_rope, L0_attn_w_kv_down, L0_attn_w_k_rope,
           L0_attn_w_uk, L0_attn_w_uv, L0_attn_w_out, L0_ffn_W, L0_ffn_b,
           L0_ffn_Wq, L0_ffn_Wk, L0_ffn_A, L0_ffn_Bm):
    x = np.asarray(L0_attn_hidden_states, dtype=np.float32)
    cos_t, sin_t = _rope_tables()

    wuvT = np.ascontiguousarray(np.asarray(L0_attn_w_uv, np.float32).T)
    WTc = np.ascontiguousarray(np.asarray(L0_ffn_W, np.float32).T)
    WqT = np.ascontiguousarray(np.asarray(L0_ffn_Wq, np.float32).T)
    WkT = np.ascontiguousarray(np.asarray(L0_ffn_Wk, np.float32).T)
    ATc = np.ascontiguousarray(
        np.asarray(L0_ffn_A, np.float32).reshape(E * R, D).T)
    Bflc = np.ascontiguousarray(
        np.asarray(L0_ffn_Bm, np.float32).transpose(0, 2, 1).reshape(E * R, D))
    bvec = np.asarray(L0_ffn_b, np.float32).reshape(1, D)

    ident = np.eye(128, dtype=np.float32)
    onesm = np.ones((128, 128), np.float32)
    ones1 = np.ones((1, 128), np.float32)
    seg = np.zeros((E * DK, E), np.float32)
    for n in range(E * DK):
        seg[n, n // DK] = 1.0
    expm = np.zeros((E, E * R), np.float32)
    for er in range(E * R):
        expm[er // R, er] = 1.0

    tri = np.tril(np.ones((128, 128), np.float32), 0)
    tri = (np.arange(128)[:, None] <= np.arange(128)[None, :]).astype(
        np.float32)

    in_maps = []
    for core in range(8):
        b = core // 2
        par = core % 2
        pos = [2 * j + par for j in range(8)] + [2 * j + 1 - par
                                                 for j in range(8)]
        xkv = np.ascontiguousarray(
            x[b].reshape(16, 128, D)[pos].reshape(S, D))

        ck2c = np.empty((128, 16 * 64), np.float32)
        sk2c = np.empty((128, 16 * 64), np.float32)
        for k in range(16):
            p = pos[k]
            cb = cos_t[p * 128:(p + 1) * 128]
            sb_ = sin_t[p * 128:(p + 1) * 128]
            ck2c[:, k * 64:k * 64 + 32] = cb
            ck2c[:, k * 64 + 32:(k + 1) * 64] = cb
            sk2c[:, k * 64:k * 64 + 32] = sb_
            sk2c[:, k * 64 + 32:(k + 1) * 64] = sb_
        cq2c = np.empty((128, NQ), np.float32)
        sq2c = np.empty((128, NQ), np.float32)
        for j in range(8):
            p = pos[j]
            cb = cos_t[p * 128:(p + 1) * 128].T
            sb_ = sin_t[p * 128:(p + 1) * 128].T
            for rep in range(4):
                cq2c[rep * 32:(rep + 1) * 32, j * 128:(j + 1) * 128] = cb
                sq2c[rep * 32:(rep + 1) * 32, j * 128:(j + 1) * 128] = sb_

        maskc = np.empty((128, 16 * 128), np.float32)
        for k in range(16):
            if k < 8:
                maskc[:, k * 128:(k + 1) * 128] = tri
            else:
                maskc[:, k * 128:(k + 1) * 128] = 0.0 if par == 0 else 1.0

        in_maps.append({
            "xkv": xkv,
            "wqd": np.asarray(L0_attn_w_q_down, np.float32),
            "wqun": np.asarray(L0_attn_w_q_up_nope, np.float32),
            "wqur": np.asarray(L0_attn_w_q_up_rope, np.float32),
            "wkvd": np.asarray(L0_attn_w_kv_down, np.float32),
            "wkr": np.asarray(L0_attn_w_k_rope, np.float32),
            "wuk": np.asarray(L0_attn_w_uk, np.float32),
            "wuvT": wuvT,
            "wout": np.asarray(L0_attn_w_out, np.float32),
            "WTi": WTc,
            "bvec": bvec,
            "WqT": WqT,
            "WkT": WkT,
            "ATi": ATc,
            "Bfl": Bflc,
            "ck2": ck2c,
            "sk2": sk2c,
            "cq2": cq2c,
            "sq2": sq2c,
            "maski": maskc,
            "identi": ident,
            "onesmi": onesm,
            "ones1i": ones1,
            "segi": seg,
            "expmi": expm,
        })

    if "nc" not in _NC_CACHE:
        _NC_CACHE["nc"] = build_nc()
    nc = _NC_CACHE["nc"]
    res = run_bass_kernel_spmd(nc, in_maps, core_ids=list(range(8)))

    out_full = np.empty((B, S, D), np.float32)
    for core in range(8):
        b = core // 2
        par = core % 2
        oc = res.results[core]["out"]
        for j in range(8):
            p = 2 * j + par
            out_full[b, p * 128:(p + 1) * 128] = oc[j * 128:(j + 1) * 128]
    return out_full
